# version 10
# baseline (speedup 1.0000x reference)
import numpy as np
import ml_dtypes
import concourse.bass as bass
import concourse.mybir as mybir
from concourse.bass_utils import run_bass_kernel_spmd

F32 = mybir.dt.float32
BF16 = mybir.dt.bfloat16
AF = mybir.ActivationFunctionType
ALU = mybir.AluOpType
AX = mybir.AxisListType
BF = ml_dtypes.bfloat16

B, N, C = 4, 4096, 2048
H = 16
NT = 2048
E, GU, IH = 32, 1536, 768
TW = 1024
NPASS = NT // TW
TC5 = TW // 512
TT1 = TW // 128
EPS = 1e-5
ENGINES = ("tensor", "vector", "scalar", "sync", "gpsimd")


class Ctr:
    __slots__ = ("h", "n")
    def __init__(self, h):
        self.h = h
        self.n = 0
    def inc(self, k=1):
        self.n += k
        return self.n


def build_nc(taps=False):
    nc = bass.Bass("TRN2", target_bir_lowering=False)

    x_my = nc.dram_tensor("x_my", [NT, C], F32, kind="ExternalInput").ap()
    x_ot = nc.dram_tensor("x_ot", [NT, C], F32, kind="ExternalInput").ap()
    wqkv = nc.dram_tensor("wqkv", [C, 3 * C], F32, kind="ExternalInput").ap()
    bqkv = nc.dram_tensor("bqkv", [3 * C], F32, kind="ExternalInput").ap()
    wout = nc.dram_tensor("wout", [C, C], F32, kind="ExternalInput").ap()
    bout = nc.dram_tensor("bout", [C], F32, kind="ExternalInput").ap()
    slopes = nc.dram_tensor("slopes", [H], F32, kind="ExternalInput").ap()
    cflag_d = nc.dram_tensor("cflag", [128, 1], F32, kind="ExternalInput").ap()
    gw_d = nc.dram_tensor("gw", [E, C], F32, kind="ExternalInput").ap()
    wgu_d = nc.dram_tensor("wgu", [E, GU, C], F32, kind="ExternalInput").ap()
    wd_d = nc.dram_tensor("wd", [E, C, IH], F32, kind="ExternalInput").ap()
    out_d = nc.dram_tensor("out", [NT, C], F32, kind="ExternalOutput").ap()
    if taps:
        tap_attn = nc.dram_tensor("tap_attn", [16 * 128, NT], F32, kind="ExternalOutput").ap()
        tap_x1 = nc.dram_tensor("tap_x1", [16 * 128, NT], F32, kind="ExternalOutput").ap()
        tap_comb = nc.dram_tensor("tap_comb", [128, 16 * 32], F32, kind="ExternalOutput").ap()
        tap_sin = nc.dram_tensor("tap_sin", [128, 16 * 128], F32, kind="ExternalOutput").ap()
        tap_q = nc.dram_tensor("tap_q", [3 * 128, NT], F32, kind="ExternalOutput").ap()

    x_bf = nc.dram_tensor("x_bf", [NT, C], BF16, kind="Internal").ap()
    wqkv_bf = nc.dram_tensor("wqkv_bf", [C, 3 * C], BF16, kind="Internal").ap()
    wout_bf = nc.dram_tensor("wout_bf", [C, C], BF16, kind="Internal").ap()
    wgu_t_d = nc.dram_tensor("wgu_t_d", [E, 128, 16 * GU], BF16, kind="Internal").ap()
    wd_t_d = nc.dram_tensor("wd_t_d", [E, 128, 6 * 2048], BF16, kind="Internal").ap()
    xo_bf = nc.dram_tensor("xo_bf", [NT, C], BF16, kind="Internal").ap()
    gw_bfd = nc.dram_tensor("gw_bfd", [E, C], BF16, kind="Internal").ap()
    wgu_bf = nc.dram_tensor("wgu_bf", [E, GU, C], BF16, kind="Internal").ap()
    wd_bf = nc.dram_tensor("wd_bf", [E, C, IH], BF16, kind="Internal").ap()
    x1t_sp = nc.dram_tensor("x1t_sp", [C, NT], BF16, kind="Internal").ap()
    x1k_sp = nc.dram_tensor("x1k_sp", [NT, C], BF16, kind="Internal").ap()

    ident_b_d = nc.inline_tensor(np.eye(128, dtype=BF), "ident_b_d").ap()
    ident_f_d = nc.inline_tensor(np.eye(128, dtype=np.float32), "ident_f_d").ap()
    ones_f_d = nc.inline_tensor(np.ones((1, 128), np.float32), "ones_f_d").ap()
    onescol_b_d = nc.inline_tensor(np.ones((128, 1), BF), "onescol_b_d").ap()

    def mk(name, shape, dt, at):
        return nc.alloc_sbuf_tensor_at(name, list(shape), dt, offset=at)

    def szof(shape, dt):
        nb = int(np.prod(shape[1:])) * (2 if dt == BF16 else 4)
        return (nb + 31) & ~31

    class Region:
        def __init__(self, base):
            self.o = base
        def al(self, name, shape, dt=F32):
            t = mk(name, shape, dt, self.o)
            self.o += szof(shape, dt)
            return t

    BASE0 = (nc.sbuf_base + 31) & ~31
    LIMIT = 224 * 1024
    P = Region(BASE0)
    ident_b = P.al("ident_b", [128, 128], BF16)
    ident_f = P.al("ident_f", [128, 128], F32)
    ones_f = P.al("ones_f", [1, 128])
    onescol_b = P.al("onescol_b", [128, 1], BF16)
    slo_row = P.al("slo_row", [1, 16])
    slo_b = P.al("slo_b", [128, 16])
    nslo_b = P.al("nslo_b", [128, 16])
    bd_b = P.al("bd_b", [128, 16])
    kdec_b = P.al("kdec_b", [128, 16])
    iota64 = P.al("iota64", [128, 64])
    iotamask = P.al("iotamask", [64, 64])
    iota8 = P.al("iota8", [128, 128])
    iotak = P.al("iotak", [128, 1])
    bq_pre = P.al("bq_pre", [48, 128])
    bq_cols = P.al("bq_cols", [128, 48])
    bo_pre = P.al("bo_pre", [16, 128])
    bo_cols = P.al("bo_cols", [128, 16])
    comb = P.al("comb", [128, 16 * 32])
    cflag = P.al("cflag_sb", [128, 1])
    OVL = P.o
    assert OVL <= BASE0 + 8 * 1024, OVL

    SIN = Region(OVL)
    s_in = SIN.al("s_in", [128, 16 * 128])
    BASE = SIN.o

    R = Region(BASE)
    xoT = [R.al(f"xoT{i}", [128, 16 * 512], BF16) for i in range(2)]
    wkvs = [R.al(f"wkvs{i}", [128, 16 * 512], BF16) for i in range(2)]
    kvf = R.al("kvf", [128, 32 * 512], BF16)
    ktokw = R.al("ktokw", [64, 8 * 128], BF16)
    vtokw = R.al("vtokw", [64, 8 * 128], BF16)
    assert R.o <= LIMIT

    A = Region(BASE)
    xT = A.al("xT", [128, 16 * 2048], BF16)
    attn = A.al("attn", [128, 16 * 2048], BF16)
    ATTN_TAIL = A.o
    qraw = A.al("qraw", [128, 2048], BF16)
    qt = A.al("qt", [128, 2048], BF16)
    kT = A.al("kT", [128, 2048], BF16)
    vT = A.al("vT", [128, 2048], BF16)
    ktok = A.al("ktok", [64, 32 * 128], BF16)
    vtok = A.al("vtok", [64, 32 * 128], BF16)
    qkvW = [A.al(f"qkvW{i}", [128, 16 * 384], BF16) for i in range(2)]
    iotag_s = A.al("iotag_s", [128, 512])
    gdec_s = A.al("gdec_s", [128, 512])
    qdec64 = A.al("qdec64", [128, 64])
    intraT = A.al("intraT", [64, 64])
    s_f = A.al("s_f", [128, 128])
    s_bf = A.al("s_bf", [128, 128], BF16)
    s_in_bf = A.al("s_in_bf", [128, 128], BF16)
    sc_sb = [A.al(f"sc_sb{i}", [64, 64], BF16) for i in range(2)]
    assert A.o <= LIMIT, A.o

    Q = Region(ATTN_TAIL)
    woutW = Q.al("woutW", [128, 16 * 1024], BF16)
    tmp_s = [Q.al(f"tmp_s{i}", [128, 512], BF16) for i in range(3)]
    r_sum = Q.al("r_sum", [1, 512])
    r_sq = Q.al("r_sq", [1, 512])
    r_mean = Q.al("r_mean", [1, 512])
    r_rstd = Q.al("r_rstd", [1, 512])
    r_mrs = Q.al("r_mrs", [1, 512])
    rstd_b = Q.al("rstd_b", [128, 512])
    mrs_b = Q.al("mrs_b", [128, 512])
    x1kstg = [Q.al(f"x1kstg{i}", [128, 2048], BF16) for i in range(2)]
    assert Q.o <= LIMIT, Q.o

    G = Region(BASE + szof([128, 16 * 2048], BF16))
    gw_sb = G.al("gw_sb", [32, 2048], BF16)
    gwt = G.al("gwt", [128, 16 * 32], BF16)
    probs = G.al("probs", [128, 16 * 32])
    probs0 = G.al("probs0", [128, 16 * 32])
    g_m = G.al("g_m", [128, 16])
    g_run = G.al("g_run", [128, 16])
    g_eq = G.al("g_eq", [128, 16 * 32])
    vs8 = G.al("vs8", [128, 16 * 8])
    ps8 = G.al("ps8", [128, 16 * 8])
    g_t1 = G.al("g_t1", [128, 16 * 32])
    g_nk = G.al("g_nk", [128, 16])
    g_vst = G.al("g_vst", [128, 16])
    g_dd = G.al("g_dd", [128, 16])

    M6 = Region(BASE)
    yacc = M6.al("yacc", [128, TT1 * 2048])
    WGUT_BASE = M6.o
    wgut = M6.al("wgut", [128, 16 * GU], BF16)
    x1w = M6.al("x1w", [128, 16 * TW], BF16)
    wdt = M6.al("wdt", [128, 6 * 2048], BF16)
    hbuf = M6.al("hbuf", [128, 6 * TW], BF16)
    gsil = [M6.al(f"gsil{i}", [128, 512], BF16) for i in range(3)]
    assert M6.o <= LIMIT, M6.o

    S7 = Region(WGUT_BASE)
    x1kt = [S7.al(f"x1kt{i}", [128, 2048], BF16) for i in range(2)]
    x2t = S7.al("x2t", [128, 2048], BF16)
    x2sq = S7.al("x2sq", [128, 2048], BF16)
    l_s1 = S7.al("l_s1", [128, 1])
    l_s2 = S7.al("l_s2", [128, 1])
    l_mean = S7.al("l_mean", [128, 1])
    l_msq = S7.al("l_msq", [128, 1])
    l_rstd = S7.al("l_rstd", [128, 1])
    outstg = [S7.al(f"outstg{i}", [128, 2048]) for i in range(2)]
    assert S7.o <= WGUT_BASE + szof([128, 16 * GU], BF16), S7.o

    STG_BASE = 229376 - 2 * 2048 - 32
    stg = [mk(f"stg{i}", [128, 1024], BF16, STG_BASE + i * 2048) for i in range(2)]
    assert A.o <= STG_BASE, (A.o, STG_BASE)
    assert Q.o <= STG_BASE, (Q.o, STG_BASE)
    assert M6.o <= STG_BASE, (M6.o, STG_BASE)

    psF = [nc.alloc_psum_tensor(f"pf{i}", [128, 512], F32) for i in range(6)]
    psT = [nc.alloc_psum_tensor(f"pt{i}", [128, 1024], BF16) for i in range(2)]

    from contextlib import ExitStack
    ctx = ExitStack()
    sem = lambda name: ctx.enter_context(nc.semaphore(name))
    qDG = Ctr(sem("qDG"))
    qDH = Ctr(sem("qDH"))
    sPE = Ctr(sem("sPE"))
    sDV = Ctr(sem("sDV"))
    sAC = Ctr(sem("sAC"))
    sGP = Ctr(sem("sGP"))
    sST = Ctr(sem("sST"))
    qDA = Ctr(sem("qDA"))

    ev = {k: [] for k in ENGINES}
    lastwait = {k: {} for k in ENGINES}

    def emit(eng, fn, waits=(), inc=None, inc_n=1):
        w = tuple(waits)
        def go(e, _fn=fn, _w=w, _i=inc, _n=inc_n, _eng=eng):
            for s, v in _w:
                lw = lastwait[_eng].get(id(s.h), -1)
                if v > lw:
                    e.wait_ge(s.h, v)
                    lastwait[_eng][id(s.h)] = v
            ins = _fn(e)
            if _i is not None:
                assert ins is not None
                ins.then_inc(_i.h, _n)
        ev[eng].append(go)
        if inc is not None:
            return inc.inc(inc_n)
        return None

    def emit_waits_only(eng, waits):
        w = tuple(waits)
        def go(e, _w=w, _eng=eng):
            for s, v in _w:
                lw = lastwait[_eng].get(id(s.h), -1)
                if v > lw:
                    e.wait_ge(s.h, v)
                    lastwait[_eng][id(s.h)] = v
        ev[eng].append(go)

    def barrier():
        marks = [(qDG, qDG.n), (qDH, qDH.n), (sPE, sPE.n), (sDV, sDV.n),
                 (sAC, sAC.n), (sGP, sGP.n), (sST, sST.n), (qDA, qDA.n)]
        for eng in ENGINES:
            emit_waits_only(eng, marks)

    T = lambda fn, waits=(), inc=sPE: emit("tensor", fn, waits=waits, inc=inc)
    V = lambda fn, waits=(), inc=sDV: emit("vector", fn, waits=waits, inc=inc)
    S = lambda fn, waits=(), inc=sAC: emit("scalar", fn, waits=waits, inc=inc)
    Y = lambda fn, waits=(), inc_n=16: emit("sync", fn, waits=waits, inc=qDH, inc_n=inc_n)
    Gp = lambda fn, waits=(), inc=sGP: emit("gpsimd", fn, waits=waits, inc=inc)
    Gd = lambda fn, waits=(): emit("gpsimd", fn, waits=waits, inc=qDG, inc_n=16)
    Sd = lambda fn, waits=(): emit("scalar", fn, waits=waits, inc=qDA, inc_n=16)

    Y(lambda e: e.dma_start(ident_b[:], ident_b_d))
    Y(lambda e: e.dma_start(ident_f[:], ident_f_d))
    Y(lambda e: e.dma_start(ones_f[:], ones_f_d))
    Y(lambda e: e.dma_start(onescol_b[:], onescol_b_d))
    Y(lambda e: e.dma_start(slo_row[:], slopes.rearrange("(x a) -> x a", x=1)))
    Y(lambda e: e.dma_start(bq_pre[:], bqkv.rearrange("(a b) -> a b", b=128)))
    Y(lambda e: e.dma_start(bo_pre[:], bout.rearrange("(a b) -> a b", b=128)))
    consts_ready = Y(lambda e: e.dma_start(cflag[:], cflag_d))

    xbf_ok = Gd(lambda e: e.dma_start(x_bf, x_my))
    xobf_ok = Gd(lambda e: e.dma_start(xo_bf, x_ot))
    gcast = Gd(lambda e: e.dma_start(gw_bfd, gw_d))

    IOK = dict(allow_small_or_imprecise_dtypes=True)
    Gp(lambda e: e.iota(iota64[:], pattern=[[1, 64]], base=0, channel_multiplier=0, **IOK))
    Gp(lambda e: e.iota(iotamask[:], pattern=[[1, 64]], base=0, channel_multiplier=-1, **IOK))
    Gp(lambda e: e.iota(iota8[:].rearrange("p (a b) -> p a b", b=8),
                        pattern=[[0, 16], [1, 8]], base=1, channel_multiplier=0, **IOK))
    Gp(lambda e: e.iota(iotak[0:64, 0:1], pattern=[[0, 1]], base=64,
                        channel_multiplier=-1, **IOK))
    io_done = Gp(lambda e: e.iota(iotak[64:128, 0:1], pattern=[[0, 1]], base=64,
                                  channel_multiplier=-1, **IOK))
    sin_z = None
    for hh in range(16):
        sin_z = Gp(lambda e, hh=hh: e.memset(s_in[:, hh * 128:(hh + 1) * 128], 0.0))

    m = T(lambda e: e.matmul(psF[0][:, 0:16], ones_f[:], slo_row[:], start=True, stop=True),
          waits=[(qDH, consts_ready)])
    v = V(lambda e: e.tensor_copy(slo_b[:], psF[0][:, 0:16]), waits=[(sPE, m)])
    slo_ok = V(lambda e: e.tensor_scalar(nslo_b[:], slo_b[:], -1.0, None, ALU.mult),
               waits=[(sDV, v)])
    bd_ok = S(lambda e: e.activation(bd_b[:], slo_b[:], AF.Exp, scale=-64.0),
              waits=[(sDV, slo_ok)])
    kd_ok = None
    for hh in range(16):
        kd_ok = S(lambda e, hh=hh: e.activation(kdec_b[:, hh:hh + 1], iotak[:], AF.Exp,
                                                scale=nslo_b[:, hh:hh + 1]),
                  waits=[(sDV, slo_ok), (sGP, io_done)])
    v = V(lambda e: e.tensor_scalar(intraT[:], iotamask[:], 0.0, None, ALU.is_lt),
          waits=[(sGP, io_done)])
    mask_ok = V(lambda e: e.scalar_tensor_tensor(iotamask[:], intraT[:], 1e5, iotamask[:],
                                                 ALU.mult, ALU.add), waits=[(sDV, v)])
    m = T(lambda e: e.transpose(psF[1][:, 0:48], bq_pre[:], ident_f[0:48, 0:48]),
          waits=[(qDH, consts_ready)])
    bq_ok = V(lambda e: e.tensor_copy(bq_cols[:], psF[1][:, 0:48]), waits=[(sPE, m)])
    m = T(lambda e: e.transpose(psF[2][:, 0:16], bo_pre[:], ident_f[0:16, 0:16]),
          waits=[(qDH, consts_ready)])
    bo_ok = V(lambda e: e.tensor_copy(bo_cols[:], psF[2][:, 0:16]), waits=[(sPE, m)])
    barrier()

    NW = 4
    kv_mm_done = [0] * NW
    xoT_ready = [0, 0]
    wkv_free = {}
    pfree = {}
    ptfree = {}
    for w in range(NW):
        wb = w % 2
        xw = [(qDG, xobf_ok)]
        if w >= 2:
            xw.append((sPE, kv_mm_done[w - 2]))
        for k in range(16):
            xoT_ready[wb] = Y(lambda e, w=w, k=k, wb=wb: e.dma_start(
                xoT[wb][:, k * 512:(k + 1) * 512],
                xo_bf[w * 512:(w + 1) * 512, k * 128:(k + 1) * 128], transpose=True),
                waits=xw)
        kv_last_ev = 0
        for g in range(8):
            gb = g % 2
            ww = [(qDG, xbf_ok)]
            if gb in wkv_free:
                ww.append(wkv_free[gb])
            wk_ok = Gd(lambda e, g=g, gb=gb: e.dma_start(
                wkvs[gb][:].rearrange("p (k m) -> p k m", m=512),
                wqkv[:, 2048 + g * 512: 2048 + (g + 1) * 512]
                .rearrange("(k p) m -> p k m", p=128)), waits=ww)
            for mt in range(4):
                slot = (g * 4 + mt) % 4
                def mmkv(e, gb=gb, mt=mt, slot=slot, wb=wb):
                    ins = None
                    for cc in range(16):
                        ins = e.matmul(
                            psF[slot][:, 0:512],
                            wkvs[gb][:, cc * 512 + mt * 128: cc * 512 + (mt + 1) * 128],
                            xoT[wb][:, cc * 512:(cc + 1) * 512],
                            start=(cc == 0), stop=(cc == 15))
                    return ins
                ww2 = [(qDG, wk_ok), (qDH, xoT_ready[wb])]
                if slot in pfree:
                    ww2.append(pfree[slot])
                mm = T(mmkv, waits=ww2)
                col = 16 + 4 * g + mt
                ek = S(lambda e, g=g, mt=mt, slot=slot, col=col: e.activation(
                    kvf[:, (4 * g + mt) * 512: (4 * g + mt + 1) * 512],
                    psF[slot][:, 0:512], AF.Identity, bias=bq_cols[:, col:col + 1]),
                    waits=[(sPE, mm), (sDV, bq_ok)])
                pfree[slot] = (sAC, ek)
                kv_last_ev = ek
            wkv_free[gb] = (sPE, sPE.n)
        kv_mm_done[w] = sPE.n
        for hh in range(16):
            tr_ev = {}
            for j8 in range(8):
                for which in range(2):
                    srccol = hh if which == 0 else 16 + hh
                    tb = (j8 * 2 + which) % 2
                    wws = [(sAC, kv_last_ev)]
                    if tb in ptfree:
                        wws.append(ptfree[tb])
                    mt_ = T(lambda e, srccol=srccol, j8=j8, tb=tb: e.transpose(
                        psT[tb][0:64, 0:128],
                        kvf[:, srccol * 512 + j8 * 64: srccol * 512 + (j8 + 1) * 64],
                        ident_b[:]), waits=wws)
                    if which == 0:
                        ee = V(lambda e, j8=j8, tb=tb, hh=hh: e.tensor_scalar(
                            ktokw[0:64, j8 * 128:(j8 + 1) * 128], psT[tb][0:64, 0:128],
                            kdec_b[0:64, hh:hh + 1], None, ALU.mult),
                            waits=[(sPE, mt_), (sAC, kd_ok)])
                        ptfree[tb] = (sDV, ee)
                        tr_ev[(j8, 0)] = (sDV, ee)
                    else:
                        ee = S(lambda e, j8=j8, tb=tb: e.copy(
                            vtokw[0:64, j8 * 128:(j8 + 1) * 128], psT[tb][0:64, 0:128]),
                            waits=[(sPE, mt_)])
                        ptfree[tb] = (sAC, ee)
                        tr_ev[(j8, 1)] = (sAC, ee)
            for j in range(8):
                wws = [tr_ev[(j, 0)], tr_ev[(j, 1)]]
                if 4 in pfree:
                    wws.append(pfree[4])
                mm = T(lambda e, j=j: e.matmul(
                    psF[4][:, 0:128],
                    ktokw[0:64, j * 128:(j + 1) * 128],
                    vtokw[0:64, j * 128:(j + 1) * 128],
                    start=True, stop=True), waits=wws)
                st_ = V(lambda e, hh=hh: e.scalar_tensor_tensor(
                    s_in[:, hh * 128:(hh + 1) * 128], s_in[:, hh * 128:(hh + 1) * 128],
                    bd_b[:, hh:hh + 1], psF[4][:, 0:128], ALU.mult, ALU.add),
                    waits=[(sPE, mm), (sGP, sin_z), (sAC, bd_ok)])
                pfree[4] = (sDV, st_)
    barrier()
    qkvW_pre = {}
    for hh0 in range(2):
        for part in range(3):
            qkvW_pre[hh0] = Gd(lambda e, hh0=hh0, part=part: e.dma_start(
                qkvW[hh0][:].rearrange("p (k m) -> p k m", m=384)[:, :, part * 128:(part + 1) * 128],
                wqkv[:, part * 2048 + hh0 * 128: part * 2048 + (hh0 + 1) * 128]
                .rearrange("(k p) m -> p k m", p=128)))
    wqkvbf_ok = Gd(lambda e: e.dma_start(wqkv_bf, wqkv))
    woutbf_ok = Gd(lambda e: e.dma_start(wout_bf, wout))

    cast_mark = {}
    wgu_t_ok = {}
    wd_t_ok = {}
    _sg = [0]
    _stg_store = [0, 0]

    def _pipe_cast(e_):
        gate = [] if e_ < 2 else [(qDH, wd_t_ok[e_ - 2])]
        Gd(lambda g, e_=e_: g.dma_start(wgu_bf[e_], wgu_d[e_]), waits=gate)
        cast_mark[e_] = Gd(lambda g, e_=e_: g.dma_start(wd_bf[e_], wd_d[e_]))

    def _pipe_wgu(e_, k, h_):
        sg_ = _sg[0]; _sg[0] ^= 1
        tmk = Y(lambda s_, e_=e_, k=k, h_=h_, sg_=sg_: s_.dma_start(
            stg[sg_][:, 0:768],
            wgu_bf[e_, h_ * 768:(h_ + 1) * 768, k * 128:(k + 1) * 128], transpose=True),
            waits=[(qDG, cast_mark[e_]), (qDH, _stg_store[sg_])])
        mkk = Y(lambda s_, e_=e_, k=k, h_=h_, sg_=sg_: s_.dma_start(
            wgu_t_d[e_][:, k * 1536 + h_ * 768: k * 1536 + (h_ + 1) * 768],
            stg[sg_][:, 0:768]), waits=[(qDH, tmk)])
        _stg_store[sg_] = mkk
        if k == 15 and h_ == 1:
            wgu_t_ok[e_] = mkk

    def _pipe_wd(e_, k6, h_):
        sg_ = _sg[0]; _sg[0] ^= 1
        tmk = Y(lambda s_, e_=e_, k6=k6, h_=h_, sg_=sg_: s_.dma_start(
            stg[sg_][:, 0:1024],
            wd_bf[e_, h_ * 1024:(h_ + 1) * 1024, k6 * 128:(k6 + 1) * 128],
            transpose=True), waits=[(qDG, cast_mark[e_]), (qDH, _stg_store[sg_])])
        mkk = Y(lambda s_, e_=e_, k6=k6, h_=h_, sg_=sg_: s_.dma_start(
            wd_t_d[e_][:, k6 * 2048 + h_ * 1024: k6 * 2048 + (h_ + 1) * 1024],
            stg[sg_][:, 0:1024]), waits=[(qDH, tmk)])
        _stg_store[sg_] = mkk
        if k6 == 5 and h_ == 1:
            wd_t_ok[e_] = mkk

    pipe_steps = []
    for e_ in range(E):
        pipe_steps.append(("cast", e_, 0))
        for k in range(16):
            for h_ in range(2):
                pipe_steps.append(("wgu", e_, k * 2 + h_))
        for k6 in range(6):
            for h_ in range(2):
                pipe_steps.append(("wd", e_, k6 * 2 + h_))
    _pp = [0]

    def pump(n):
        while n > 0 and _pp[0] < len(pipe_steps):
            kind, a1, a2 = pipe_steps[_pp[0]]
            _pp[0] += 1
            if kind == "cast":
                _pipe_cast(a1)
            elif kind == "wgu":
                _pipe_wgu(a1, a2 // 2, a2 % 2)
                n -= 1
            else:
                _pipe_wd(a1, a2 // 2, a2 % 2)
                n -= 1

    if taps:
        emit("gpsimd", lambda e: e.dma_start(
            tap_sin.rearrange("p (k t) -> p k t", t=128),
            s_in[:].rearrange("p (k t) -> p k t", t=128)), inc=qDG, inc_n=16)

    xt_ready = 0
    for k in range(16):
        xt_ready = Y(lambda e, k=k: e.dma_start(
            xT[:, k * 2048:(k + 1) * 2048],
            x_bf[:, k * 128:(k + 1) * 128], transpose=True),
            waits=[(qDG, xbf_ok)])

    pfree = {}
    ptfree = {}
    qkvW_free = {}
    prev_qt_reader = 0
    prev_sc_reader = 0
    prev_gd_reader = 0
    prev_corr_pe = 0
    head_marks = []
    for hh in range(16):
        hb = hh % 2
        if hh < 2:
            wq_ok = qkvW_pre[hh]
            wq_q = qDG
        else:
            ww = [(qDG, wqkvbf_ok)]
            if hb in qkvW_free:
                ww.append(qkvW_free[hb])
            wq_ok = None
            for part in range(3):
                wq_ok = Sd(lambda e, hh=hh, hb=hb, part=part: e.dma_start(
                    qkvW[hb][:].rearrange("p (k m) -> p k m", m=384)[:, :, part * 128:(part + 1) * 128],
                    wqkv_bf[:, part * 2048 + hh * 128: part * 2048 + (hh + 1) * 128]
                    .rearrange("(k p) m -> p k m", p=128)), waits=ww)
            wq_q = qDA
        ev_done = {}
        for part, dstbuf in ((0, qraw), (1, kT), (2, vT)):
            for tc in range(4):
                slot = (part * 4 + tc) % 3
                def mmq(e, hb=hb, part=part, tc=tc, slot=slot):
                    ins = None
                    for cc in range(16):
                        ins = e.matmul(
                            psF[slot][:, 0:512],
                            qkvW[hb][:, cc * 384 + part * 128: cc * 384 + (part + 1) * 128],
                            xT[:, cc * 2048 + tc * 512: cc * 2048 + tc * 512 + 512],
                            start=(cc == 0), stop=(cc == 15))
                    return ins
                wws = [(wq_q, wq_ok), (qDH, xt_ready)]
                if slot in pfree:
                    wws.append(pfree[slot])
                mm = T(mmq, waits=wws)
                col = part * 16 + hh
                ee = S(lambda e, dstbuf=dstbuf, tc=tc, slot=slot, col=col:
                       e.activation(dstbuf[:, tc * 512:(tc + 1) * 512], psF[slot][:, 0:512],
                                    AF.Identity, bias=bq_cols[:, col:col + 1]),
                       waits=[(sPE, mm), (sDV, bq_ok)])
                pfree[slot] = (sAC, ee)
                ev_done[(part, tc)] = ee
        qkvW_free[hb] = (sPE, sPE.n)
        q_ok = max(ev_done[(0, tc)] for tc in range(4))
        k_ok = max(ev_done[(1, tc)] for tc in range(4))
        v_ok = max(ev_done[(2, tc)] for tc in range(4))
        qd_ok = S(lambda e, hh=hh: e.activation(qdec64[:], iota64[:], AF.Exp,
                                                scale=nslo_b[:, hh:hh + 1]),
                  waits=[(sDV, max(slo_ok, prev_qt_reader)), (sGP, io_done)])
        it_ok = S(lambda e, hh=hh: e.activation(intraT[:], iotamask[:], AF.Exp,
                                                scale=nslo_b[0:64, hh:hh + 1]),
                  waits=[(sDV, max(mask_ok, prev_sc_reader))])
        qt_ok = V(lambda e: e.tensor_mul(
            qt[:].rearrange("p (r x) -> p r x", x=64),
            qraw[:].rearrange("p (r x) -> p r x", x=64),
            qdec64[:].unsqueeze(1).broadcast_to([128, 32, 64])),
            waits=[(sAC, max(q_ok, qd_ok)), (sPE, prev_corr_pe)])
        prev_qt_reader = qt_ok
        tr_k = {}
        tr_v = {}
        for c32 in range(32):
            for which in range(2):
                srcb = kT if which == 0 else vT
                tb = (c32 * 2 + which) % 2
                wws = [(sAC, k_ok if which == 0 else v_ok)]
                if tb in ptfree:
                    wws.append(ptfree[tb])
                mt_ = T(lambda e, srcb=srcb, c32=c32, tb=tb: e.transpose(
                    psT[tb][0:64, 0:128], srcb[:, c32 * 64:(c32 + 1) * 64], ident_b[:]),
                    waits=wws)
                if which == 0:
                    ee = V(lambda e, c32=c32, tb=tb, hh=hh: e.tensor_scalar(
                        ktok[0:64, c32 * 128:(c32 + 1) * 128], psT[tb][0:64, 0:128],
                        kdec_b[0:64, hh:hh + 1], None, ALU.mult),
                        waits=[(sPE, mt_), (sAC, kd_ok)])
                    ptfree[tb] = (sDV, ee)
                    tr_k[c32] = (sDV, ee)
                else:
                    ee = S(lambda e, c32=c32, tb=tb: e.copy(
                        vtok[0:64, c32 * 128:(c32 + 1) * 128], psT[tb][0:64, 0:128]),
                        waits=[(sPE, mt_)])
                    ptfree[tb] = (sAC, ee)
                    tr_v[c32] = (sAC, ee)
        sz = V(lambda e: e.memset(s_f[:], 0.0), waits=[(sDV, sDV.n)])
        sbf_mark = 0
        sc_free = [None, None]
        o_free = pfree.get(1)
        sA_free = pfree.get(2)
        s_free = pfree.get(0)
        for cch in range(32):
            cb = cch % 2
            wws = [(sAC, it_ok), (sAC, max(k_ok, q_ok)), (sDV, qt_ok)]
            if sc_free[cb] is not None:
                wws.append(sc_free[cb])
            if s_free is not None:
                wws.append(s_free)
            mm_s = T(lambda e, cch=cch: e.matmul(
                psF[0][0:64, 0:64],
                kT[:, cch * 64:(cch + 1) * 64], qraw[:, cch * 64:(cch + 1) * 64],
                start=True, stop=True), waits=wws)
            ev_s = V(lambda e, cb=cb: e.tensor_mul(sc_sb[cb][:], psF[0][0:64, 0:64],
                                                   intraT[:]), waits=[(sPE, mm_s)])
            s_free = (sDV, ev_s)
            prev_sc_reader = ev_s
            def mm_o_fn(e, cb=cb, cch=cch):
                ins = e.matmul(psF[1][:, 0:64],
                               vtok[0:64, cch * 128:(cch + 1) * 128],
                               sc_sb[cb][:], start=True, stop=(cch == 0))
                if cch > 0:
                    ins = e.matmul(psF[1][:, 0:64], s_bf[:],
                                   qt[:, cch * 64:(cch + 1) * 64], start=False, stop=True)
                return ins
            wws = [(sDV, ev_s), tr_v[cch]]
            if o_free is not None:
                wws.append(o_free)
            if cch > 0:
                wws.append((sDV, sbf_mark))
            mm_o = T(mm_o_fn, waits=wws)
            sc_free[cb] = (sPE, mm_o)
            ev_o = S(lambda e, hh=hh, cch=cch: e.copy(
                attn[:, hh * 2048 + cch * 64: hh * 2048 + (cch + 1) * 64],
                psF[1][:, 0:64]), waits=[(sPE, mm_o)])
            o_free = (sAC, ev_o)
            wws = [tr_k[cch], tr_v[cch]]
            if sA_free is not None:
                wws.append(sA_free)
            mm_a = T(lambda e, cch=cch: e.matmul(
                psF[2][:, 0:128],
                ktok[0:64, cch * 128:(cch + 1) * 128],
                vtok[0:64, cch * 128:(cch + 1) * 128],
                start=True, stop=True), waits=wws)
            st_ = V(lambda e, hh=hh: e.scalar_tensor_tensor(
                s_f[:], s_f[:], bd_b[:, hh:hh + 1], psF[2][:, 0:128], ALU.mult, ALU.add),
                waits=[(sPE, mm_a), (sDV, sz), (sAC, bd_ok)])
            sA_free = (sDV, st_)
            if cch < 31:
                sbf_mark = V(lambda e: e.tensor_copy(s_bf[:], s_f[:]), waits=[(sDV, st_)])
        pfree[0] = s_free
        pfree[1] = o_free
        pfree[2] = sA_free
        sinb_ok = V(lambda e, hh=hh: e.tensor_scalar(
            s_in_bf[:], s_in[:, hh * 128:(hh + 1) * 128], cflag[:, 0:1], None, ALU.mult),
            waits=[(sPE, prev_corr_pe), (qDH, consts_ready)])
        scan_pe_done = sPE.n
        for tc in range(4):
            gio = Gp(lambda e, tc=tc: e.iota(iotag_s[:], pattern=[[1, 512]], base=tc * 512,
                                             channel_multiplier=0, **IOK),
                     waits=[(sAC, sAC.n), (sDV, prev_gd_reader)])
            gde = S(lambda e, hh=hh: e.activation(gdec_s[:], iotag_s[:], AF.Exp,
                                                  scale=nslo_b[:, hh:hh + 1]),
                    waits=[(sGP, gio), (sDV, prev_gd_reader)])
            qh_ok = V(lambda e, tc=tc: e.tensor_mul(
                qt[:, tc * 512:(tc + 1) * 512], qraw[:, tc * 512:(tc + 1) * 512],
                gdec_s[:]), waits=[(sAC, gde), (sPE, scan_pe_done)])
            prev_gd_reader = qh_ok
            slot = 3 + tc % 2
            wws = [(sDV, qh_ok), (sDV, sinb_ok)]
            if slot in pfree:
                wws.append(pfree[slot])
            mm = T(lambda e, tc=tc, slot=slot: e.matmul(
                psF[slot][:, 0:512], s_in_bf[:], qt[:, tc * 512:(tc + 1) * 512],
                start=True, stop=True), waits=wws)
            cadd = V(lambda e, hh=hh, tc=tc, slot=slot: e.tensor_add(
                attn[:, hh * 2048 + tc * 512: hh * 2048 + (tc + 1) * 512],
                attn[:, hh * 2048 + tc * 512: hh * 2048 + (tc + 1) * 512],
                psF[slot][:, 0:512]),
                waits=[(sPE, mm), (sAC, sAC.n)])
            pfree[slot] = (sDV, cadd)
        prev_corr_pe = sPE.n
        head_marks.append(sPE.n)
    barrier()
    if taps:
        emit("gpsimd", lambda e: e.dma_start(
            tap_attn.rearrange("(k p) t -> p k t", p=128),
            attn[:].rearrange("p (k t) -> p k t", t=2048)), inc=qDG, inc_n=16)
        emit("gpsimd", lambda e: e.dma_start(
            tap_q[0:128, :], qraw[:]), inc=qDG, inc_n=16)
        emit("gpsimd", lambda e: e.dma_start(
            tap_q[128:256, :], kT[:]), inc=qDG, inc_n=16)
        emit("gpsimd", lambda e: e.dma_start(
            tap_q[256:384, :], vT[:]), inc=qDG, inc_n=16)
        barrier()

    pfree = {}
    tmp_free = {}
    x1pre_marks = {}
    for half in range(2):
        wo_ok = None
        wws = [(qDG, woutbf_ok)] if half == 0 else [(qDG, woutbf_ok), (sPE, sPE.n)]
        for cc in range(16):
            wo_ok = Sd(lambda e, cc=cc, half=half: e.dma_start(
                woutW[:, cc * 1024:(cc + 1) * 1024],
                wout_bf[cc * 128:(cc + 1) * 128, half * 1024:(half + 1) * 1024]), waits=wws)
        pump(44)
        for ml in range(8):
            mtile = half * 8 + ml
            for tc in range(4):
                slot = (ml * 4 + tc) % 4
                def mmw(e, ml=ml, tc=tc, slot=slot):
                    ins = None
                    for cc in range(16):
                        ins = e.matmul(
                            psF[slot][:, 0:512],
                            woutW[:, cc * 1024 + ml * 128: cc * 1024 + (ml + 1) * 128],
                            attn[:, cc * 2048 + tc * 512: cc * 2048 + tc * 512 + 512],
                            start=(cc == 0), stop=(cc == 15))
                    return ins
                wws2 = [(qDA, wo_ok)]
                if slot in pfree:
                    wws2.append(pfree[slot])
                mm = T(mmw, waits=wws2)
                tb = (mtile * 4 + tc) % 3
                wws3 = [(sPE, mm), (sDV, bo_ok)]
                if tb in tmp_free:
                    wws3.append(tmp_free[tb])
                ee = S(lambda e, tb=tb, slot=slot, mtile=mtile: e.activation(
                    tmp_s[tb][:], psF[slot][:, 0:512], AF.Identity,
                    bias=bo_cols[:, mtile:mtile + 1]), waits=wws3)
                pfree[slot] = (sAC, ee)
                ra = V(lambda e, mtile=mtile, tc=tc, tb=tb: e.tensor_add(
                    xT[:, mtile * 2048 + tc * 512: mtile * 2048 + tc * 512 + 512],
                    xT[:, mtile * 2048 + tc * 512: mtile * 2048 + tc * 512 + 512],
                    tmp_s[tb][:]), waits=[(sAC, ee)])
                tmp_free[tb] = (sDV, ra)
                x1pre_marks[(mtile, tc)] = ra
    x1_done = 0
    for tc in range(4):
        pre_ok = max(x1pre_marks[(m_, tc)] for m_ in range(16))
        def mm_sum(e, tc=tc):
            ins = None
            for cc in range(16):
                ins = e.matmul(psF[4][0:1, 0:512], onescol_b[:],
                               xT[:, cc * 2048 + tc * 512: cc * 2048 + tc * 512 + 512],
                               start=(cc == 0), stop=(cc == 15))
            return ins
        wws = [(sDV, pre_ok), (qDH, consts_ready)]
        if 4 in pfree:
            wws.append(pfree[4])
        mms = T(mm_sum, waits=wws)
        ev1 = V(lambda e: e.tensor_copy(r_sum[:], psF[4][0:1, 0:512]), waits=[(sPE, mms)])
        pfree[4] = (sDV, ev1)
        sq_marks = []
        for cc in range(16):
            tb = cc % 3
            wws = [(sDV, pre_ok)]
            if tb in tmp_free:
                wws.append(tmp_free[tb])
            if cc >= 3:
                wws.append((sPE, sq_marks[cc - 3]))
            sq = V(lambda e, cc=cc, tc=tc, tb=tb: e.tensor_mul(
                tmp_s[tb][:],
                xT[:, cc * 2048 + tc * 512: cc * 2048 + tc * 512 + 512],
                xT[:, cc * 2048 + tc * 512: cc * 2048 + tc * 512 + 512]), waits=wws)
            wws = [(sDV, sq)]
            if cc == 0 and 5 in pfree:
                wws.append(pfree[5])
            mq = T(lambda e, tb=tb, cc=cc: e.matmul(
                psF[5][0:1, 0:512], onescol_b[:], tmp_s[tb][:],
                start=(cc == 0), stop=(cc == 15)), waits=wws)
            tmp_free[tb] = (sPE, mq)
            sq_marks.append(mq)
        ev2 = V(lambda e: e.tensor_copy(r_sq[:], psF[5][0:1, 0:512]),
                waits=[(sPE, sq_marks[-1])])
        pfree[5] = (sDV, ev2)
        v1 = V(lambda e: e.tensor_scalar(r_mean[:], r_sum[:], 1.0 / 2048, None, ALU.mult),
               waits=[(sDV, ev1)])
        v2 = V(lambda e: e.tensor_scalar(r_sq[:], r_sq[:], 1.0 / 2048, None, ALU.mult),
               waits=[(sDV, ev2)])
        v3 = V(lambda e: e.tensor_mul(r_mrs[:], r_mean[:], r_mean[:]), waits=[(sDV, v1)])
        v4 = V(lambda e: e.tensor_sub(r_rstd[:], r_sq[:], r_mrs[:]),
               waits=[(sDV, v2), (sDV, v3)])
        v4b = V(lambda e: e.tensor_scalar(r_rstd[:], r_rstd[:], EPS, None, ALU.add),
                waits=[(sDV, v4)])
        a1 = S(lambda e: e.activation(r_rstd[:], r_rstd[:], AF.Ln),
               waits=[(sDV, v4b)])
        a2 = S(lambda e: e.activation(r_rstd[:], r_rstd[:], AF.Exp, scale=-0.5),
               waits=[(sAC, a1)])
        v5 = V(lambda e: e.tensor_mul(r_mrs[:], r_mean[:], r_rstd[:]), waits=[(sAC, a2)])
        wws = [(sAC, a2), pfree[4]]
        mb1 = T(lambda e: e.matmul(psF[4][:, 0:512], ones_f[:], r_rstd[:],
                                   start=True, stop=True), waits=wws)
        eb1 = V(lambda e: e.tensor_copy(rstd_b[:], psF[4][:, 0:512]), waits=[(sPE, mb1)])
        pfree[4] = (sDV, eb1)
        wws = [(sDV, v5), pfree[5]]
        mb2 = T(lambda e: e.matmul(psF[5][:, 0:512], ones_f[:], r_mrs[:],
                                   start=True, stop=True), waits=wws)
        eb2 = V(lambda e: e.tensor_copy(mrs_b[:], psF[5][:, 0:512]), waits=[(sPE, mb2)])
        pfree[5] = (sDV, eb2)
        for cc in range(16):
            n1 = V(lambda e, cc=cc, tc=tc: e.tensor_mul(
                xT[:, cc * 2048 + tc * 512: cc * 2048 + tc * 512 + 512],
                xT[:, cc * 2048 + tc * 512: cc * 2048 + tc * 512 + 512],
                rstd_b[:]), waits=[(sDV, eb1)])
            x1_done = V(lambda e, cc=cc, tc=tc: e.tensor_sub(
                xT[:, cc * 2048 + tc * 512: cc * 2048 + tc * 512 + 512],
                xT[:, cc * 2048 + tc * 512: cc * 2048 + tc * 512 + 512],
                mrs_b[:]), waits=[(sDV, n1), (sDV, eb2)])
    barrier()

    sp1 = Y(lambda e: e.dma_start(
        x1t_sp.rearrange("(k p) t -> p k t", p=128),
        xT[:].rearrange("p (k t) -> p k t", t=2048)))
    ptfree = {}
    stg_sent = [0, 0]
    x1k_ok = 0
    for ttile in range(16):
        sb_ = ttile % 2
        last_ee = 0
        for cc in range(16):
            tb = cc % 2
            wws = []
            if tb in ptfree:
                wws.append(ptfree[tb])
            if cc == 0 and stg_sent[sb_]:
                wws.append((sST, stg_sent[sb_]))
            mt_ = T(lambda e, cc=cc, ttile=ttile, tb=tb: e.transpose(
                psT[tb][:, 0:128],
                xT[:, cc * 2048 + ttile * 128: cc * 2048 + (ttile + 1) * 128],
                ident_b[:]), waits=wws)
            last_ee = S(lambda e, cc=cc, tb=tb, sb_=sb_: e.copy(
                x1kstg[sb_][:, cc * 128:(cc + 1) * 128], psT[tb][:, 0:128]),
                waits=[(sPE, mt_)])
            ptfree[tb] = (sAC, last_ee)
        snt = emit("sync", lambda e, ttile=ttile, sb_=sb_: e.dma_start(
            x1k_sp[ttile * 128:(ttile + 1) * 128, :], x1kstg[sb_][:]),
            waits=[(sAC, last_ee)], inc=sST, inc_n=16)
        stg_sent[sb_] = snt
        x1k_ok = snt

    gld = Y(lambda e: e.dma_start(gw_sb[:], gw_bfd), waits=[(qDG, gcast)])
    gwt_ok = 0
    for k in range(16):
        tb = k % 2
        wws = [(qDH, gld)]
        if tb in ptfree:
            wws.append(ptfree[tb])
        mt_ = T(lambda e, k=k, tb=tb: e.transpose(
            psT[tb][:, 0:32], gw_sb[0:32, k * 128:(k + 1) * 128], ident_b[0:32, 0:32]),
            waits=wws)
        gwt_ok = S(lambda e, k=k, tb=tb: e.copy(
            gwt[:, k * 32:(k + 1) * 32], psT[tb][:, 0:32]), waits=[(sPE, mt_)])
        ptfree[tb] = (sAC, gwt_ok)
    lg_ok = 0
    for ttile in range(16):
        slot = ttile % 4
        def mml(e, ttile=ttile, slot=slot):
            ins = None
            for cc in range(16):
                ins = e.matmul(psF[slot][:, 0:32],
                               xT[:, cc * 2048 + ttile * 128: cc * 2048 + (ttile + 1) * 128],
                               gwt[:, cc * 32:(cc + 1) * 32],
                               start=(cc == 0), stop=(cc == 15))
            return ins
        wws = [(sAC, gwt_ok), (sDV, x1_done)]
        if slot in pfree:
            wws.append(pfree[slot])
        mm = T(mml, waits=wws)
        lg_ok = S(lambda e, ttile=ttile, slot=slot: e.copy(
            probs[:, ttile * 32:(ttile + 1) * 32], psF[slot][:, 0:32]),
            waits=[(sPE, mm)])
        pfree[slot] = (sAC, lg_ok)
    NTL = 16
    mx_ok = 0
    for tt in range(NTL):
        mx_ok = V(lambda e, tt=tt: e.reduce_max(g_m[:, tt:tt + 1],
                                                probs[:, tt * 32:(tt + 1) * 32], axis=AX.X),
                  waits=[(sAC, lg_ok)])
    sub_ok = 0
    for tt in range(NTL):
        sub_ok = V(lambda e, tt=tt: e.tensor_scalar(
            probs[:, tt * 32:(tt + 1) * 32], probs[:, tt * 32:(tt + 1) * 32],
            g_m[:, tt:tt + 1], None, ALU.subtract), waits=[(sDV, mx_ok)])
    a0 = S(lambda e: e.activation(probs[:], probs[:], AF.Exp), waits=[(sDV, sub_ok)])
    sm_ok = 0
    for tt in range(NTL):
        sm_ok = V(lambda e, tt=tt: e.reduce_sum(g_m[:, tt:tt + 1],
                                                probs[:, tt * 32:(tt + 1) * 32], axis=AX.X),
                  waits=[(sAC, a0)])
    rc_ok = V(lambda e: e.reciprocal(g_run[:], g_m[:]), waits=[(sDV, sm_ok)])
    nm_ok = 0
    for tt in range(NTL):
        nm_ok = V(lambda e, tt=tt: e.tensor_scalar(
            probs[:, tt * 32:(tt + 1) * 32], probs[:, tt * 32:(tt + 1) * 32],
            g_run[:, tt:tt + 1], None, ALU.mult), waits=[(sDV, rc_ok)])
    w5 = V(lambda e: e.tensor_copy(probs0[:], probs[:]), waits=[(sDV, nm_ok)])
    wz = V(lambda e: e.memset(g_run[:], 0.0), waits=[(sDV, nm_ok)])
    last = max(w5, wz)
    for k8 in range(8):
        m0 = 0
        for tt in range(NTL):
            m0 = V(lambda e, tt=tt: e.reduce_max(g_m[:, tt:tt + 1],
                                                 probs[:, tt * 32:(tt + 1) * 32], axis=AX.X),
                   waits=[(sDV, last)])
        m1 = V(lambda e, k8=k8: e.tensor_copy(vs8[:, k8 * 16:(k8 + 1) * 16], g_m[:]),
               waits=[(sDV, m0)])
        m2 = V(lambda e: e.tensor_add(g_run[:], g_run[:], g_m[:]), waits=[(sDV, m0)])
        m3 = V(lambda e, k8=k8: e.tensor_copy(ps8[:, k8 * 16:(k8 + 1) * 16], g_run[:]),
               waits=[(sDV, m2)])
        last = max(m1, m3)
        if k8 < 7:
            eq_ok = 0
            for tt in range(NTL):
                eq_ok = V(lambda e, tt=tt: e.tensor_scalar(
                    g_eq[:, tt * 32:(tt + 1) * 32], probs[:, tt * 32:(tt + 1) * 32],
                    g_m[:, tt:tt + 1], None, ALU.is_equal), waits=[(sDV, m1)])
            m5 = V(lambda e: e.scalar_tensor_tensor(probs[:], g_eq[:], -1e30, probs[:],
                                                    ALU.mult, ALU.add), waits=[(sDV, eq_ok)])
            last = max(last, m5)
    t0 = V(lambda e: e.tensor_scalar(g_m[:], ps8[:, 7 * 16:8 * 16], 0.8, None, ALU.mult),
           waits=[(sDV, last)])
    tz = V(lambda e: e.memset(g_nk[:], 1.0), waits=[(sDV, last)])
    nk_ok = tz
    for k8 in range(7):
        c_ = V(lambda e, k8=k8: e.tensor_tensor(g_dd[:], ps8[:, k8 * 16:(k8 + 1) * 16],
                                                g_m[:], ALU.is_lt), waits=[(sDV, t0), (sDV, nk_ok)])
        nk_ok = V(lambda e: e.tensor_add(g_nk[:], g_nk[:], g_dd[:]), waits=[(sDV, c_)])
    vz = V(lambda e: e.memset(g_vst[:], 0.0), waits=[(sDV, last)])
    dz = V(lambda e: e.memset(g_dd[:], 0.0), waits=[(sDV, nk_ok)])
    vst_ok, dd_ok = vz, dz
    for k8 in range(8):
        oh = V(lambda e, k8=k8: e.tensor_scalar(g_t1[:, 0:16], g_nk[:], float(k8 + 1), None,
                                                ALU.is_equal), waits=[(sDV, nk_ok), (sDV, vst_ok), (sDV, dd_ok)])
        p1 = V(lambda e, k8=k8: e.tensor_mul(g_t1[:, 16:32], vs8[:, k8 * 16:(k8 + 1) * 16],
                                             g_t1[:, 0:16]), waits=[(sDV, oh)])
        vst_ok = V(lambda e: e.tensor_add(g_vst[:], g_vst[:], g_t1[:, 16:32]),
                   waits=[(sDV, p1)])
        p2 = V(lambda e, k8=k8: e.tensor_mul(g_t1[:, 32:48], ps8[:, k8 * 16:(k8 + 1) * 16],
                                             g_t1[:, 0:16]), waits=[(sDV, oh)])
        dd_ok = V(lambda e: e.tensor_add(g_dd[:], g_dd[:], g_t1[:, 32:48]),
                  waits=[(sDV, p2)])
    rd_ok = V(lambda e: e.reciprocal(g_nk[:], g_dd[:]), waits=[(sDV, dd_ok)])
    ge_ok = 0
    for tt in range(NTL):
        ge_ok = V(lambda e, tt=tt: e.tensor_scalar(
            g_eq[:, tt * 32:(tt + 1) * 32], probs0[:, tt * 32:(tt + 1) * 32],
            g_vst[:, tt:tt + 1], None, ALU.is_ge), waits=[(sDV, vst_ok), (sDV, w5)])
    tb_ = V(lambda e: e.tensor_mul(comb[:], probs0[:], g_eq[:]), waits=[(sDV, ge_ok)])
    comb_ok = 0
    for tt in range(NTL):
        comb_ok = V(lambda e, tt=tt: e.tensor_scalar(
            comb[:, tt * 32:(tt + 1) * 32], comb[:, tt * 32:(tt + 1) * 32],
            g_nk[:, tt:tt + 1], None, ALU.mult), waits=[(sDV, tb_), (sDV, rd_ok)])
    barrier()

    gu_done_pe = 0
    down_done_pe = 0
    x1w_users_pe = 0
    prev_pass_p7 = None
    out_sent_mark = 0
    x1kt_used = [0, 0]
    out_busy = [0, 0]
    pfree = {}
    for p_ in range(NPASS):
        wws = [(qDH, sp1)]
        if p_ > 0:
            wws.append((sPE, x1w_users_pe))
        x1w_ok = Y(lambda e, p_=p_: e.dma_start(
            x1w[:].rearrange("p (k t) -> p k t", t=TW),
            x1t_sp.rearrange("(k p) t -> p k t", p=128)[:, :, p_ * TW:(p_ + 1) * TW]),
            waits=wws)
        yz = None
        for m_ in range(TT1):
            wws = []
            if prev_pass_p7 is not None:
                wws = [prev_pass_p7]
            yz = Gp(lambda e, m_=m_: e.memset(yacc[:, m_ * 2048:(m_ + 1) * 2048], 0.0),
                    waits=wws)
        for e_ in range(E):
            pump(44)
            wws = [(qDH, max(wgu_t_ok[e_], gu_done_pe_q))] if False else []
            wws = [(qDH, wgu_t_ok[e_]), (sPE, gu_done_pe)]
            if prev_pass_p7 is not None and e_ == 0:
                wws = wws + [(sST, out_sent_mark), prev_pass_p7]
            wg_ev = Y(lambda s_, e_=e_: s_.dma_start(wgut[:], wgu_t_d[e_]), waits=wws)
            wws = [(qDH, wd_t_ok[e_]), (sPE, down_done_pe)]
            wd_ev = Y(lambda s_, e_=e_: s_.dma_start(wdt[:], wd_t_d[e_]), waits=wws)
            h_ok = 0
            for iu in range(6):
                for tc in range(TC5):
                    slot = (iu * TC5 + tc) % 3
                    def mmg(t_, iu=iu, tc=tc, slot=slot):
                        ins = None
                        for cc in range(16):
                            ins = t_.matmul(
                                psF[slot][:, 0:512],
                                wgut[:, cc * GU + iu * 128: cc * GU + (iu + 1) * 128],
                                x1w[:, cc * TW + tc * 512: cc * TW + tc * 512 + 512],
                                start=(cc == 0), stop=(cc == 15))
                        return ins
                    wws = [(qDH, max(wg_ev, x1w_ok))]
                    if slot in pfree:
                        wws.append(pfree[slot])
                    mmgm = T(mmg, waits=wws)
                    gb = (iu * TC5 + tc) % 3
                    sil = S(lambda s_, gb=gb, slot=slot: s_.activation(
                        gsil[gb][:], psF[slot][:, 0:512], AF.Silu), waits=[(sPE, mmgm)])
                    pfree[slot] = (sAC, sil)
                    slot2 = 3 + (iu * TC5 + tc) % 3
                    def mmu(t_, iu=iu, tc=tc, slot2=slot2):
                        ins = None
                        for cc in range(16):
                            ins = t_.matmul(
                                psF[slot2][:, 0:512],
                                wgut[:, cc * GU + (6 + iu) * 128: cc * GU + (7 + iu) * 128],
                                x1w[:, cc * TW + tc * 512: cc * TW + tc * 512 + 512],
                                start=(cc == 0), stop=(cc == 15))
                        return ins
                    wws = [(qDH, max(wg_ev, x1w_ok))]
                    if slot2 in pfree:
                        wws.append(pfree[slot2])
                    mmum = T(mmu, waits=wws)
                    ue = S(lambda s_, iu=iu, tc=tc, slot2=slot2: s_.copy(
                        hbuf[:, iu * TW + tc * 512: iu * TW + tc * 512 + 512],
                        psF[slot2][:, 0:512]), waits=[(sPE, mmum)])
                    h_ok = V(lambda v_, iu=iu, tc=tc, gb=gb: v_.tensor_mul(
                        hbuf[:, iu * TW + tc * 512: iu * TW + tc * 512 + 512],
                        hbuf[:, iu * TW + tc * 512: iu * TW + tc * 512 + 512],
                        gsil[gb][:]), waits=[(sAC, max(ue, sil))])
                    pfree[slot2] = (sDV, h_ok)
            gu_done_pe = sPE.n
            x1w_users_pe = sPE.n
            for ttile in range(TT1):
                for jc in range(4):
                    slot = 3 + (ttile * 4 + jc) % 3
                    def mmd(t_, ttile=ttile, jc=jc, slot=slot):
                        ins = None
                        for ic in range(6):
                            ins = t_.matmul(
                                psF[slot][:, 0:512],
                                hbuf[:, ic * TW + ttile * 128: ic * TW + (ttile + 1) * 128],
                                wdt[:, ic * 2048 + jc * 512: ic * 2048 + jc * 512 + 512],
                                start=(ic == 0), stop=(ic == 5))
                        return ins
                    wws = [(sDV, h_ok), (qDH, wd_ev)]
                    if slot in pfree:
                        wws.append(pfree[slot])
                    mmdm = T(mmd, waits=wws)
                    ttg = p_ * TT1 + ttile
                    ya = V(lambda v_, ttile=ttile, jc=jc, slot=slot, ttg=ttg, e_=e_:
                           v_.scalar_tensor_tensor(
                               yacc[:, ttile * 2048 + jc * 512: ttile * 2048 + jc * 512 + 512],
                               psF[slot][:, 0:512],
                               comb[:, ttg * 32 + e_: ttg * 32 + e_ + 1],
                               yacc[:, ttile * 2048 + jc * 512: ttile * 2048 + jc * 512 + 512],
                               ALU.mult, ALU.add),
                           waits=[(sPE, mmdm), (sGP, yz), (sDV, comb_ok)])
                    pfree[slot] = (sDV, ya)
            down_done_pe = sPE.n
        p7_last = 0
        for ttile in range(TT1):
            tb = ttile % 2
            wws = [(sST, x1k_ok), (sPE, gu_done_pe)]
            if x1kt_used[tb]:
                wws.append((sDV, x1kt_used[tb]))
            ld = Y(lambda e, p_=p_, ttile=ttile, tb=tb: e.dma_start(
                x1kt[tb][:], x1k_sp[p_ * TW + ttile * 128: p_ * TW + (ttile + 1) * 128, :]),
                waits=wws)
            yready = sDV.n
            c1 = V(lambda e, ttile=ttile, tb=tb: e.tensor_add(
                x2t[:], yacc[:, ttile * 2048:(ttile + 1) * 2048], x1kt[tb][:]),
                waits=[(qDH, ld), (sDV, yready)])
            x1kt_used[tb] = c1
            c2 = V(lambda e: e.reduce_sum(l_s1[:], x2t[:], axis=AX.X), waits=[(sDV, c1)])
            c3 = V(lambda e: e.tensor_mul(x2sq[:], x2t[:], x2t[:]), waits=[(sDV, c1)])
            c4 = V(lambda e: e.reduce_sum(l_s2[:], x2sq[:], axis=AX.X), waits=[(sDV, c3)])
            c5 = V(lambda e: e.tensor_scalar(l_mean[:], l_s1[:], 1.0 / 2048, None, ALU.mult),
                   waits=[(sDV, c2)])
            c6 = V(lambda e: e.tensor_mul(l_msq[:], l_mean[:], l_mean[:]), waits=[(sDV, c5)])
            c7 = V(lambda e: e.tensor_scalar(l_rstd[:], l_s2[:], 1.0 / 2048, None, ALU.mult),
                   waits=[(sDV, c4)])
            c8 = V(lambda e: e.tensor_sub(l_rstd[:], l_rstd[:], l_msq[:]),
                   waits=[(sDV, c7), (sDV, c6)])
            c8b = V(lambda e: e.tensor_scalar(l_rstd[:], l_rstd[:], EPS, None, ALU.add),
                    waits=[(sDV, c8)])
            c9 = S(lambda e: e.activation(l_rstd[:], l_rstd[:], AF.Ln),
                   waits=[(sDV, c8b)])
            ca = S(lambda e: e.activation(l_rstd[:], l_rstd[:], AF.Exp, scale=-0.5),
                   waits=[(sAC, c9)])
            ob = ttile % 2
            wws = [(sDV, c1), (sAC, ca)]
            if out_busy[ob]:
                wws.append((sST, out_busy[ob]))
            cb_ = V(lambda e, ob=ob: e.tensor_scalar(
                outstg[ob][:], x2t[:], l_mean[:, 0:1], l_rstd[:, 0:1],
                ALU.subtract, ALU.mult), waits=wws)
            snt = emit("sync", lambda e, p_=p_, ttile=ttile, ob=ob: e.dma_start(
                out_d[p_ * TW + ttile * 128: p_ * TW + (ttile + 1) * 128, :], outstg[ob][:]),
                waits=[(sDV, cb_)], inc=sST, inc_n=16)
            out_busy[ob] = snt
            out_sent_mark = snt
            p7_last = cb_
        prev_pass_p7 = (sDV, p7_last)

    emit_waits_only("sync", [(sST, sST.n)])

    block = ctx.enter_context(nc.Block())
    block.tensor(lambda t: [f(t) for f in ev["tensor"]])
    block.vector(lambda v: [f(v) for f in ev["vector"]])
    block.scalar(lambda s: [f(s) for f in ev["scalar"]])
    block.sync(lambda s: [f(s) for f in ev["sync"]])
    block.gpsimd(lambda g: [f(g) for f in ev["gpsimd"]])
    ctx.close()
    return nc


_NC_CACHE = None


def kernel(L0_attn_x, L0_attn_Wqkv, L0_attn_bqkv, L0_attn_Wout, L0_attn_bout,
           L0_attn_slopes, L0_ffn_gate_weight, L0_ffn_gate_up_proj,
           L0_ffn_down_proj):
    global _NC_CACHE
    x = np.ascontiguousarray(np.asarray(L0_attn_x, np.float32))
    wqkv = np.ascontiguousarray(np.asarray(L0_attn_Wqkv, np.float32))
    bqkv = np.ascontiguousarray(np.asarray(L0_attn_bqkv, np.float32))
    wout = np.ascontiguousarray(np.asarray(L0_attn_Wout, np.float32))
    bout = np.ascontiguousarray(np.asarray(L0_attn_bout, np.float32))
    slopes = np.ascontiguousarray(np.asarray(L0_attn_slopes, np.float32))
    gw = np.ascontiguousarray(np.asarray(L0_ffn_gate_weight, np.float32))
    wgu = np.ascontiguousarray(np.asarray(L0_ffn_gate_up_proj, np.float32))
    wd = np.ascontiguousarray(np.asarray(L0_ffn_down_proj, np.float32))

    if _NC_CACHE is None:
        _NC_CACHE = build_nc()
    nc = _NC_CACHE

    zeros = np.zeros((NT, C), np.float32)
    flag0 = np.zeros((128, 1), np.float32)
    flag1 = np.ones((128, 1), np.float32)
    in_maps = []
    for c in range(8):
        b, half = c // 2, c % 2
        x_mine = np.ascontiguousarray(x[b, half * NT:(half + 1) * NT, :])
        x_other = np.ascontiguousarray(x[b, 0:NT, :]) if half == 1 else zeros
        in_maps.append(dict(x_my=x_mine, x_ot=x_other, wqkv=wqkv, bqkv=bqkv,
                            wout=wout, bout=bout, slopes=slopes, gw=gw,
                            wgu=wgu, wd=wd, cflag=flag1 if half == 1 else flag0))
    res = run_bass_kernel_spmd(nc, in_maps, core_ids=list(range(8)))
    out = np.empty((B, N, C), np.float32)
    for c in range(8):
        b, half = c // 2, c % 2
        out[b, half * NT:(half + 1) * NT, :] = res.results[c]["out"]
    return out


# version 12
# speedup vs baseline: 1.1397x; 1.1397x over previous
import numpy as np
import ml_dtypes
import concourse.bass as bass
import concourse.mybir as mybir
from concourse.bass_utils import run_bass_kernel_spmd

F32 = mybir.dt.float32
BF16 = mybir.dt.bfloat16
AF = mybir.ActivationFunctionType
ALU = mybir.AluOpType
AX = mybir.AxisListType
BF = ml_dtypes.bfloat16

B, N, C = 4, 4096, 2048
H = 16
NT = 2048
E, GU, IH = 32, 1536, 768
TW = 1024
NPASS = NT // TW
TC5 = TW // 512
TT1 = TW // 128
EPS = 1e-5
ENGINES = ("tensor", "vector", "scalar", "sync", "gpsimd")


class Ctr:
    __slots__ = ("h", "n")
    def __init__(self, h):
        self.h = h
        self.n = 0
    def inc(self, k=1):
        self.n += k
        return self.n


def build_nc(taps=False):
    nc = bass.Bass("TRN2", target_bir_lowering=False)

    x_my = nc.dram_tensor("x_my", [NT, C], F32, kind="ExternalInput").ap()
    x_ot = nc.dram_tensor("x_ot", [NT, C], F32, kind="ExternalInput").ap()
    wqkv = nc.dram_tensor("wqkv", [C, 3 * C], F32, kind="ExternalInput").ap()
    bqkv = nc.dram_tensor("bqkv", [3 * C], F32, kind="ExternalInput").ap()
    wout = nc.dram_tensor("wout", [C, C], F32, kind="ExternalInput").ap()
    bout = nc.dram_tensor("bout", [C], F32, kind="ExternalInput").ap()
    slopes = nc.dram_tensor("slopes", [H], F32, kind="ExternalInput").ap()
    cflag_d = nc.dram_tensor("cflag", [128, 1], F32, kind="ExternalInput").ap()
    gw_d = nc.dram_tensor("gw", [E, C], F32, kind="ExternalInput").ap()
    wgu_d = nc.dram_tensor("wgu", [E, GU, C], F32, kind="ExternalInput").ap()
    wd_d = nc.dram_tensor("wd", [E, C, IH], F32, kind="ExternalInput").ap()
    out_d = nc.dram_tensor("out", [NT, C], F32, kind="ExternalOutput").ap()
    if taps:
        tap_attn = nc.dram_tensor("tap_attn", [16 * 128, NT], F32, kind="ExternalOutput").ap()
        tap_x1 = nc.dram_tensor("tap_x1", [16 * 128, NT], F32, kind="ExternalOutput").ap()
        tap_comb = nc.dram_tensor("tap_comb", [128, 16 * 32], F32, kind="ExternalOutput").ap()
        tap_sin = nc.dram_tensor("tap_sin", [128, 16 * 128], F32, kind="ExternalOutput").ap()
        tap_q = nc.dram_tensor("tap_q", [3 * 128, NT], F32, kind="ExternalOutput").ap()

    x_bf = nc.dram_tensor("x_bf", [NT, C], BF16, kind="Internal").ap()
    wqkv_bf = nc.dram_tensor("wqkv_bf", [C, 3 * C], BF16, kind="Internal").ap()
    wout_bf = nc.dram_tensor("wout_bf", [C, C], BF16, kind="Internal").ap()
    wgu_t_d = nc.dram_tensor("wgu_t_d", [E, 128, 16 * GU], BF16, kind="Internal").ap()
    wd_t_d = nc.dram_tensor("wd_t_d", [E, 128, 6 * 2048], BF16, kind="Internal").ap()
    xo_bf = nc.dram_tensor("xo_bf", [NT, C], BF16, kind="Internal").ap()
    gw_bfd = nc.dram_tensor("gw_bfd", [E, C], BF16, kind="Internal").ap()
    wgu_bf = nc.dram_tensor("wgu_bf", [E, GU, C], BF16, kind="Internal").ap()
    wd_bf = nc.dram_tensor("wd_bf", [E, C, IH], BF16, kind="Internal").ap()
    x1t_sp = nc.dram_tensor("x1t_sp", [C, NT], BF16, kind="Internal").ap()
    x1k_sp = nc.dram_tensor("x1k_sp", [NT, C], BF16, kind="Internal").ap()

    ident_b_d = nc.inline_tensor(np.eye(128, dtype=BF), "ident_b_d").ap()
    ident_f_d = nc.inline_tensor(np.eye(128, dtype=np.float32), "ident_f_d").ap()
    ones_f_d = nc.inline_tensor(np.ones((1, 128), np.float32), "ones_f_d").ap()
    onescol_b_d = nc.inline_tensor(np.ones((128, 1), BF), "onescol_b_d").ap()

    def mk(name, shape, dt, at):
        return nc.alloc_sbuf_tensor_at(name, list(shape), dt, offset=at)

    def szof(shape, dt):
        nb = int(np.prod(shape[1:])) * (2 if dt == BF16 else 4)
        return (nb + 31) & ~31

    class Region:
        def __init__(self, base):
            self.o = base
        def al(self, name, shape, dt=F32):
            t = mk(name, shape, dt, self.o)
            self.o += szof(shape, dt)
            return t

    BASE0 = (nc.sbuf_base + 31) & ~31
    LIMIT = 224 * 1024
    P = Region(BASE0)
    ident_b = P.al("ident_b", [128, 128], BF16)
    ident_f = P.al("ident_f", [128, 128], F32)
    ones_f = P.al("ones_f", [1, 128])
    onescol_b = P.al("onescol_b", [128, 1], BF16)
    slo_row = P.al("slo_row", [1, 16])
    slo_b = P.al("slo_b", [128, 16])
    nslo_b = P.al("nslo_b", [128, 16])
    bd_b = P.al("bd_b", [128, 16])
    kdec_b = P.al("kdec_b", [128, 16])
    iota64 = P.al("iota64", [128, 64])
    iotamask = P.al("iotamask", [64, 64])
    iota8 = P.al("iota8", [128, 128])
    iotak = P.al("iotak", [128, 1])
    bq_pre = P.al("bq_pre", [48, 128])
    bq_cols = P.al("bq_cols", [128, 48])
    bo_pre = P.al("bo_pre", [16, 128])
    bo_cols = P.al("bo_cols", [128, 16])
    comb = P.al("comb", [128, 16 * 32])
    cflag = P.al("cflag_sb", [128, 1])
    OVL = P.o
    assert OVL <= BASE0 + 8 * 1024, OVL

    SIN = Region(OVL)
    s_in = SIN.al("s_in", [128, 16 * 128])
    BASE = SIN.o

    R = Region(BASE)
    xoT = [R.al(f"xoT{i}", [128, 16 * 512], BF16) for i in range(2)]
    wkvs = [R.al(f"wkvs{i}", [128, 16 * 512], BF16) for i in range(2)]
    kvf = R.al("kvf", [128, 32 * 512], BF16)
    ktokw = R.al("ktokw", [64, 8 * 128], BF16)
    vtokw = R.al("vtokw", [64, 8 * 128], BF16)
    assert R.o <= LIMIT

    A = Region(BASE)
    xT = A.al("xT", [128, 16 * 2048], BF16)
    attn = A.al("attn", [128, 16 * 2048], BF16)
    ATTN_TAIL = A.o
    qraw = A.al("qraw", [128, 2048], BF16)
    qt = A.al("qt", [128, 2048], BF16)
    kT = A.al("kT", [128, 2048], BF16)
    vT = A.al("vT", [128, 2048], BF16)
    ktok = A.al("ktok", [64, 32 * 128], BF16)
    vtok = A.al("vtok", [64, 32 * 128], BF16)
    qkvW = [A.al(f"qkvW{i}", [128, 16 * 384], BF16) for i in range(2)]
    iotag_s = A.al("iotag_s", [128, 512])
    gdec_s = A.al("gdec_s", [128, 512])
    qdec64 = A.al("qdec64", [128, 64])
    intraT = A.al("intraT", [64, 64])
    s_f = A.al("s_f", [128, 128])
    s_bf = A.al("s_bf", [128, 128], BF16)
    s_in_bf = A.al("s_in_bf", [128, 128], BF16)
    sc_sb = [A.al(f"sc_sb{i}", [64, 64], BF16) for i in range(2)]
    assert A.o <= LIMIT, A.o

    Q = Region(ATTN_TAIL)
    woutW = Q.al("woutW", [128, 16 * 1024], BF16)
    tmp_s = [Q.al(f"tmp_s{i}", [128, 512], BF16) for i in range(3)]
    r_sum = Q.al("r_sum", [1, 512])
    r_sq = Q.al("r_sq", [1, 512])
    r_mean = Q.al("r_mean", [1, 512])
    r_rstd = Q.al("r_rstd", [1, 512])
    r_mrs = Q.al("r_mrs", [1, 512])
    rstd_b = Q.al("rstd_b", [128, 512])
    mrs_b = Q.al("mrs_b", [128, 512])
    x1kstg = [Q.al(f"x1kstg{i}", [128, 2048], BF16) for i in range(2)]
    assert Q.o <= LIMIT, Q.o

    G = Region(BASE + szof([128, 16 * 2048], BF16))
    gw_sb = G.al("gw_sb", [32, 2048], BF16)
    gwt = G.al("gwt", [128, 16 * 32], BF16)
    probs = G.al("probs", [128, 16 * 32])
    probs0 = G.al("probs0", [128, 16 * 32])
    g_m = G.al("g_m", [128, 16])
    g_run = G.al("g_run", [128, 16])
    g_eq = G.al("g_eq", [128, 16 * 32])
    vs8 = G.al("vs8", [128, 16 * 8])
    ps8 = G.al("ps8", [128, 16 * 8])
    g_t1 = G.al("g_t1", [128, 16 * 32])
    g_nk = G.al("g_nk", [128, 16])
    g_vst = G.al("g_vst", [128, 16])
    g_dd = G.al("g_dd", [128, 16])

    M6 = Region(BASE)
    yacc = M6.al("yacc", [128, TT1 * 2048])
    WGUT_BASE = M6.o
    wgut = M6.al("wgut", [128, 16 * GU], BF16)
    x1w = M6.al("x1w", [128, 16 * TW], BF16)
    wdt = M6.al("wdt", [128, 6 * 2048], BF16)
    hbuf = M6.al("hbuf", [128, 6 * TW], BF16)
    gsilf = M6.al("gsilf", [128, 6 * TW], BF16)
    assert M6.o <= LIMIT, M6.o

    S7 = Region(WGUT_BASE)
    x1kt = [S7.al(f"x1kt{i}", [128, 2048], BF16) for i in range(2)]
    x2t = S7.al("x2t", [128, 2048], BF16)
    x2sq = S7.al("x2sq", [128, 2048], BF16)
    l_s1 = S7.al("l_s1", [128, 1])
    l_s2 = S7.al("l_s2", [128, 1])
    l_mean = S7.al("l_mean", [128, 1])
    l_msq = S7.al("l_msq", [128, 1])
    l_rstd = S7.al("l_rstd", [128, 1])
    outstg = [S7.al(f"outstg{i}", [128, 2048]) for i in range(2)]
    assert S7.o <= WGUT_BASE + szof([128, 16 * GU], BF16), S7.o

    psF = [nc.alloc_psum_tensor(f"pf{i}", [128, 512], F32) for i in range(6)]
    psT = [nc.alloc_psum_tensor(f"pt{i}", [128, 1024], BF16) for i in range(2)]

    from contextlib import ExitStack
    ctx = ExitStack()
    sem = lambda name: ctx.enter_context(nc.semaphore(name))
    qDG = Ctr(sem("qDG"))
    qDH = Ctr(sem("qDH"))
    sPE = Ctr(sem("sPE"))
    sDV = Ctr(sem("sDV"))
    sAC = Ctr(sem("sAC"))
    sGP = Ctr(sem("sGP"))
    sST = Ctr(sem("sST"))
    qDA = Ctr(sem("qDA"))

    ev = {k: [] for k in ENGINES}
    lastwait = {k: {} for k in ENGINES}

    def emit(eng, fn, waits=(), inc=None, inc_n=1):
        w = tuple(waits)
        def go(e, _fn=fn, _w=w, _i=inc, _n=inc_n, _eng=eng):
            for s, v in _w:
                lw = lastwait[_eng].get(id(s.h), -1)
                if v > lw:
                    e.wait_ge(s.h, v)
                    lastwait[_eng][id(s.h)] = v
            ins = _fn(e)
            if _i is not None:
                assert ins is not None
                ins.then_inc(_i.h, _n)
        ev[eng].append(go)
        if inc is not None:
            return inc.inc(inc_n)
        return None

    def emit_waits_only(eng, waits):
        w = tuple(waits)
        def go(e, _w=w, _eng=eng):
            for s, v in _w:
                lw = lastwait[_eng].get(id(s.h), -1)
                if v > lw:
                    e.wait_ge(s.h, v)
                    lastwait[_eng][id(s.h)] = v
        ev[eng].append(go)

    def barrier():
        marks = [(qDG, qDG.n), (qDH, qDH.n), (sPE, sPE.n), (sDV, sDV.n),
                 (sAC, sAC.n), (sGP, sGP.n), (sST, sST.n), (qDA, qDA.n)]
        for eng in ENGINES:
            emit_waits_only(eng, marks)

    T = lambda fn, waits=(), inc=sPE: emit("tensor", fn, waits=waits, inc=inc)
    V = lambda fn, waits=(), inc=sDV: emit("vector", fn, waits=waits, inc=inc)
    S = lambda fn, waits=(), inc=sAC: emit("scalar", fn, waits=waits, inc=inc)
    Y = lambda fn, waits=(), inc_n=16: emit("sync", fn, waits=waits, inc=qDH, inc_n=inc_n)
    Gp = lambda fn, waits=(), inc=sGP: emit("gpsimd", fn, waits=waits, inc=inc)
    Gd = lambda fn, waits=(): emit("gpsimd", fn, waits=waits, inc=qDG, inc_n=16)
    Sd = lambda fn, waits=(): emit("scalar", fn, waits=waits, inc=qDA, inc_n=16)

    Y(lambda e: e.dma_start(ident_b[:], ident_b_d))
    Y(lambda e: e.dma_start(ident_f[:], ident_f_d))
    Y(lambda e: e.dma_start(ones_f[:], ones_f_d))
    Y(lambda e: e.dma_start(onescol_b[:], onescol_b_d))
    Y(lambda e: e.dma_start(slo_row[:], slopes.rearrange("(x a) -> x a", x=1)))
    Y(lambda e: e.dma_start(bq_pre[:], bqkv.rearrange("(a b) -> a b", b=128)))
    Y(lambda e: e.dma_start(bo_pre[:], bout.rearrange("(a b) -> a b", b=128)))
    consts_ready = Y(lambda e: e.dma_start(cflag[:], cflag_d))

    xbf_ok = Gd(lambda e: e.dma_start(x_bf, x_my))
    xobf_ok = Gd(lambda e: e.dma_start(xo_bf, x_ot))
    gcast = Gd(lambda e: e.dma_start(gw_bfd, gw_d))

    IOK = dict(allow_small_or_imprecise_dtypes=True)
    Gp(lambda e: e.iota(iota64[:], pattern=[[1, 64]], base=0, channel_multiplier=0, **IOK))
    Gp(lambda e: e.iota(iotamask[:], pattern=[[1, 64]], base=0, channel_multiplier=-1, **IOK))
    Gp(lambda e: e.iota(iota8[:].rearrange("p (a b) -> p a b", b=8),
                        pattern=[[0, 16], [1, 8]], base=1, channel_multiplier=0, **IOK))
    Gp(lambda e: e.iota(iotak[0:64, 0:1], pattern=[[0, 1]], base=64,
                        channel_multiplier=-1, **IOK))
    io_done = Gp(lambda e: e.iota(iotak[64:128, 0:1], pattern=[[0, 1]], base=64,
                                  channel_multiplier=-1, **IOK))
    sin_z = None
    for hh in range(16):
        sin_z = Gp(lambda e, hh=hh: e.memset(s_in[:, hh * 128:(hh + 1) * 128], 0.0))

    m = T(lambda e: e.matmul(psF[0][:, 0:16], ones_f[:], slo_row[:], start=True, stop=True),
          waits=[(qDH, consts_ready)])
    v = V(lambda e: e.tensor_copy(slo_b[:], psF[0][:, 0:16]), waits=[(sPE, m)])
    slo_ok = V(lambda e: e.tensor_scalar(nslo_b[:], slo_b[:], -1.0, None, ALU.mult),
               waits=[(sDV, v)])
    bd_ok = S(lambda e: e.activation(bd_b[:], slo_b[:], AF.Exp, scale=-64.0),
              waits=[(sDV, slo_ok)])
    kd_ok = None
    for hh in range(16):
        kd_ok = S(lambda e, hh=hh: e.activation(kdec_b[:, hh:hh + 1], iotak[:], AF.Exp,
                                                scale=nslo_b[:, hh:hh + 1]),
                  waits=[(sDV, slo_ok), (sGP, io_done)])
    v = V(lambda e: e.tensor_scalar(intraT[:], iotamask[:], 0.0, None, ALU.is_lt),
          waits=[(sGP, io_done)])
    mask_ok = V(lambda e: e.scalar_tensor_tensor(iotamask[:], intraT[:], 1e5, iotamask[:],
                                                 ALU.mult, ALU.add), waits=[(sDV, v)])
    m = T(lambda e: e.transpose(psF[1][:, 0:48], bq_pre[:], ident_f[0:48, 0:48]),
          waits=[(qDH, consts_ready)])
    bq_ok = V(lambda e: e.tensor_copy(bq_cols[:], psF[1][:, 0:48]), waits=[(sPE, m)])
    m = T(lambda e: e.transpose(psF[2][:, 0:16], bo_pre[:], ident_f[0:16, 0:16]),
          waits=[(qDH, consts_ready)])
    bo_ok = V(lambda e: e.tensor_copy(bo_cols[:], psF[2][:, 0:16]), waits=[(sPE, m)])
    barrier()

    NW = 4
    kv_mm_done = [0] * NW
    xoT_ready = [0, 0]
    wkv_free = {}
    pfree = {}
    ptfree = {}
    for w in range(NW):
        wb = w % 2
        xw = [(qDG, xobf_ok)]
        if w >= 2:
            xw.append((sPE, kv_mm_done[w - 2]))
        for k in range(16):
            xoT_ready[wb] = Y(lambda e, w=w, k=k, wb=wb: e.dma_start(
                xoT[wb][:, k * 512:(k + 1) * 512],
                xo_bf[w * 512:(w + 1) * 512, k * 128:(k + 1) * 128], transpose=True),
                waits=xw)
        kv_last_ev = 0
        for g in range(8):
            gb = g % 2
            ww = [(qDG, xbf_ok)]
            if gb in wkv_free:
                ww.append(wkv_free[gb])
            wk_ok = Gd(lambda e, g=g, gb=gb: e.dma_start(
                wkvs[gb][:].rearrange("p (k m) -> p k m", m=512),
                wqkv[:, 2048 + g * 512: 2048 + (g + 1) * 512]
                .rearrange("(k p) m -> p k m", p=128)), waits=ww)
            for mt in range(4):
                slot = (g * 4 + mt) % 4
                def mmkv(e, gb=gb, mt=mt, slot=slot, wb=wb):
                    ins = None
                    for cc in range(16):
                        ins = e.matmul(
                            psF[slot][:, 0:512],
                            wkvs[gb][:, cc * 512 + mt * 128: cc * 512 + (mt + 1) * 128],
                            xoT[wb][:, cc * 512:(cc + 1) * 512],
                            start=(cc == 0), stop=(cc == 15))
                    return ins
                ww2 = [(qDG, wk_ok), (qDH, xoT_ready[wb])]
                if slot in pfree:
                    ww2.append(pfree[slot])
                mm = T(mmkv, waits=ww2)
                col = 16 + 4 * g + mt
                ek = S(lambda e, g=g, mt=mt, slot=slot, col=col: e.activation(
                    kvf[:, (4 * g + mt) * 512: (4 * g + mt + 1) * 512],
                    psF[slot][:, 0:512], AF.Identity, bias=bq_cols[:, col:col + 1]),
                    waits=[(sPE, mm), (sDV, bq_ok)])
                pfree[slot] = (sAC, ek)
                kv_last_ev = ek
            wkv_free[gb] = (sPE, sPE.n)
        kv_mm_done[w] = sPE.n
        for hh in range(16):
            tr_ev = {}
            for j8 in range(8):
                for which in range(2):
                    srccol = hh if which == 0 else 16 + hh
                    tb = (j8 * 2 + which) % 2
                    wws = [(sAC, kv_last_ev)]
                    if tb in ptfree:
                        wws.append(ptfree[tb])
                    mt_ = T(lambda e, srccol=srccol, j8=j8, tb=tb: e.transpose(
                        psT[tb][0:64, 0:128],
                        kvf[:, srccol * 512 + j8 * 64: srccol * 512 + (j8 + 1) * 64],
                        ident_b[:]), waits=wws)
                    if which == 0:
                        ee = V(lambda e, j8=j8, tb=tb, hh=hh: e.tensor_scalar(
                            ktokw[0:64, j8 * 128:(j8 + 1) * 128], psT[tb][0:64, 0:128],
                            kdec_b[0:64, hh:hh + 1], None, ALU.mult),
                            waits=[(sPE, mt_), (sAC, kd_ok)])
                        ptfree[tb] = (sDV, ee)
                        tr_ev[(j8, 0)] = (sDV, ee)
                    else:
                        ee = S(lambda e, j8=j8, tb=tb: e.copy(
                            vtokw[0:64, j8 * 128:(j8 + 1) * 128], psT[tb][0:64, 0:128]),
                            waits=[(sPE, mt_)])
                        ptfree[tb] = (sAC, ee)
                        tr_ev[(j8, 1)] = (sAC, ee)
            for j in range(8):
                wws = [tr_ev[(j, 0)], tr_ev[(j, 1)]]
                if 4 in pfree:
                    wws.append(pfree[4])
                mm = T(lambda e, j=j: e.matmul(
                    psF[4][:, 0:128],
                    ktokw[0:64, j * 128:(j + 1) * 128],
                    vtokw[0:64, j * 128:(j + 1) * 128],
                    start=True, stop=True), waits=wws)
                st_ = V(lambda e, hh=hh: e.scalar_tensor_tensor(
                    s_in[:, hh * 128:(hh + 1) * 128], s_in[:, hh * 128:(hh + 1) * 128],
                    bd_b[:, hh:hh + 1], psF[4][:, 0:128], ALU.mult, ALU.add),
                    waits=[(sPE, mm), (sGP, sin_z), (sAC, bd_ok)])
                pfree[4] = (sDV, st_)
    barrier()
    qkvW_pre = {}
    for hh0 in range(2):
        for part in range(3):
            qkvW_pre[hh0] = Gd(lambda e, hh0=hh0, part=part: e.dma_start(
                qkvW[hh0][:].rearrange("p (k m) -> p k m", m=384)[:, :, part * 128:(part + 1) * 128],
                wqkv[:, part * 2048 + hh0 * 128: part * 2048 + (hh0 + 1) * 128]
                .rearrange("(k p) m -> p k m", p=128)))
    wqkvbf_ok = Gd(lambda e: e.dma_start(wqkv_bf, wqkv))
    woutbf_ok = Gd(lambda e: e.dma_start(wout_bf, wout))

    if taps:
        emit("gpsimd", lambda e: e.dma_start(
            tap_sin.rearrange("p (k t) -> p k t", t=128),
            s_in[:].rearrange("p (k t) -> p k t", t=128)), inc=qDG, inc_n=16)

    xt_ready = 0
    for k in range(16):
        xt_ready = Y(lambda e, k=k: e.dma_start(
            xT[:, k * 2048:(k + 1) * 2048],
            x_bf[:, k * 128:(k + 1) * 128], transpose=True),
            waits=[(qDG, xbf_ok)])

    pfree = {}
    ptfree = {}
    qkvW_free = {}
    prev_qt_reader = 0
    prev_sc_reader = 0
    prev_gd_reader = 0
    prev_corr_pe = 0
    for hh in range(16):
        hb = hh % 2
        if hh < 2:
            wq_ok = qkvW_pre[hh]
            wq_q = qDG
        else:
            ww = [(qDG, wqkvbf_ok)]
            if hb in qkvW_free:
                ww.append(qkvW_free[hb])
            wq_ok = None
            for part in range(3):
                wq_ok = Sd(lambda e, hh=hh, hb=hb, part=part: e.dma_start(
                    qkvW[hb][:].rearrange("p (k m) -> p k m", m=384)[:, :, part * 128:(part + 1) * 128],
                    wqkv_bf[:, part * 2048 + hh * 128: part * 2048 + (hh + 1) * 128]
                    .rearrange("(k p) m -> p k m", p=128)), waits=ww)
            wq_q = qDA
        ev_done = {}
        for part, dstbuf in ((0, qraw), (1, kT), (2, vT)):
            for tc in range(4):
                slot = (part * 4 + tc) % 3
                def mmq(e, hb=hb, part=part, tc=tc, slot=slot):
                    ins = None
                    for cc in range(16):
                        ins = e.matmul(
                            psF[slot][:, 0:512],
                            qkvW[hb][:, cc * 384 + part * 128: cc * 384 + (part + 1) * 128],
                            xT[:, cc * 2048 + tc * 512: cc * 2048 + tc * 512 + 512],
                            start=(cc == 0), stop=(cc == 15))
                    return ins
                wws = [(wq_q, wq_ok), (qDH, xt_ready)]
                if slot in pfree:
                    wws.append(pfree[slot])
                mm = T(mmq, waits=wws)
                col = part * 16 + hh
                ee = S(lambda e, dstbuf=dstbuf, tc=tc, slot=slot, col=col:
                       e.activation(dstbuf[:, tc * 512:(tc + 1) * 512], psF[slot][:, 0:512],
                                    AF.Identity, bias=bq_cols[:, col:col + 1]),
                       waits=[(sPE, mm), (sDV, bq_ok)])
                pfree[slot] = (sAC, ee)
                ev_done[(part, tc)] = ee
        qkvW_free[hb] = (sPE, sPE.n)
        q_ok = max(ev_done[(0, tc)] for tc in range(4))
        k_ok = max(ev_done[(1, tc)] for tc in range(4))
        v_ok = max(ev_done[(2, tc)] for tc in range(4))
        qd_ok = S(lambda e, hh=hh: e.activation(qdec64[:], iota64[:], AF.Exp,
                                                scale=nslo_b[:, hh:hh + 1]),
                  waits=[(sDV, max(slo_ok, prev_qt_reader)), (sGP, io_done)])
        it_ok = S(lambda e, hh=hh: e.activation(intraT[:], iotamask[:], AF.Exp,
                                                scale=nslo_b[0:64, hh:hh + 1]),
                  waits=[(sDV, max(mask_ok, prev_sc_reader))])
        qt_ok = V(lambda e: e.tensor_mul(
            qt[:].rearrange("p (r x) -> p r x", x=64),
            qraw[:].rearrange("p (r x) -> p r x", x=64),
            qdec64[:].unsqueeze(1).broadcast_to([128, 32, 64])),
            waits=[(sAC, max(q_ok, qd_ok)), (sPE, prev_corr_pe)])
        prev_qt_reader = qt_ok
        tr_k = {}
        tr_v = {}
        for c32 in range(32):
            for which in range(2):
                srcb = kT if which == 0 else vT
                tb = (c32 * 2 + which) % 2
                wws = [(sAC, k_ok if which == 0 else v_ok)]
                if tb in ptfree:
                    wws.append(ptfree[tb])
                mt_ = T(lambda e, srcb=srcb, c32=c32, tb=tb: e.transpose(
                    psT[tb][0:64, 0:128], srcb[:, c32 * 64:(c32 + 1) * 64], ident_b[:]),
                    waits=wws)
                if which == 0:
                    ee = V(lambda e, c32=c32, tb=tb, hh=hh: e.tensor_scalar(
                        ktok[0:64, c32 * 128:(c32 + 1) * 128], psT[tb][0:64, 0:128],
                        kdec_b[0:64, hh:hh + 1], None, ALU.mult),
                        waits=[(sPE, mt_), (sAC, kd_ok)])
                    ptfree[tb] = (sDV, ee)
                    tr_k[c32] = (sDV, ee)
                else:
                    ee = S(lambda e, c32=c32, tb=tb: e.copy(
                        vtok[0:64, c32 * 128:(c32 + 1) * 128], psT[tb][0:64, 0:128]),
                        waits=[(sPE, mt_)])
                    ptfree[tb] = (sAC, ee)
                    tr_v[c32] = (sAC, ee)
        sz = V(lambda e: e.memset(s_f[:], 0.0), waits=[(sDV, sDV.n)])
        sbf_mark = 0
        sc_free = [None, None]
        o_free = pfree.get(1)
        sA_free = pfree.get(2)
        s_free = pfree.get(0)
        for cch in range(32):
            cb = cch % 2
            wws = [(sAC, it_ok), (sAC, max(k_ok, q_ok)), (sDV, qt_ok)]
            if sc_free[cb] is not None:
                wws.append(sc_free[cb])
            if s_free is not None:
                wws.append(s_free)
            mm_s = T(lambda e, cch=cch: e.matmul(
                psF[0][0:64, 0:64],
                kT[:, cch * 64:(cch + 1) * 64], qraw[:, cch * 64:(cch + 1) * 64],
                start=True, stop=True), waits=wws)
            ev_s = V(lambda e, cb=cb: e.tensor_mul(sc_sb[cb][:], psF[0][0:64, 0:64],
                                                   intraT[:]), waits=[(sPE, mm_s)])
            s_free = (sDV, ev_s)
            prev_sc_reader = ev_s
            def mm_o_fn(e, cb=cb, cch=cch):
                ins = e.matmul(psF[1][:, 0:64],
                               vtok[0:64, cch * 128:(cch + 1) * 128],
                               sc_sb[cb][:], start=True, stop=(cch == 0))
                if cch > 0:
                    ins = e.matmul(psF[1][:, 0:64], s_bf[:],
                                   qt[:, cch * 64:(cch + 1) * 64], start=False, stop=True)
                return ins
            wws = [(sDV, ev_s), tr_v[cch]]
            if o_free is not None:
                wws.append(o_free)
            if cch > 0:
                wws.append((sDV, sbf_mark))
            mm_o = T(mm_o_fn, waits=wws)
            sc_free[cb] = (sPE, mm_o)
            ev_o = S(lambda e, hh=hh, cch=cch: e.copy(
                attn[:, hh * 2048 + cch * 64: hh * 2048 + (cch + 1) * 64],
                psF[1][:, 0:64]), waits=[(sPE, mm_o)])
            o_free = (sAC, ev_o)
            wws = [tr_k[cch], tr_v[cch]]
            if sA_free is not None:
                wws.append(sA_free)
            mm_a = T(lambda e, cch=cch: e.matmul(
                psF[2][:, 0:128],
                ktok[0:64, cch * 128:(cch + 1) * 128],
                vtok[0:64, cch * 128:(cch + 1) * 128],
                start=True, stop=True), waits=wws)
            st_ = V(lambda e, hh=hh: e.scalar_tensor_tensor(
                s_f[:], s_f[:], bd_b[:, hh:hh + 1], psF[2][:, 0:128], ALU.mult, ALU.add),
                waits=[(sPE, mm_a), (sDV, sz), (sAC, bd_ok)])
            sA_free = (sDV, st_)
            if cch < 31:
                sbf_mark = V(lambda e: e.tensor_copy(s_bf[:], s_f[:]), waits=[(sDV, st_)])
        pfree[0] = s_free
        pfree[1] = o_free
        pfree[2] = sA_free
        sinb_ok = V(lambda e, hh=hh: e.tensor_scalar(
            s_in_bf[:], s_in[:, hh * 128:(hh + 1) * 128], cflag[:, 0:1], None, ALU.mult),
            waits=[(sPE, prev_corr_pe), (qDH, consts_ready)])
        scan_pe_done = sPE.n
        for tc in range(4):
            gio = Gp(lambda e, tc=tc: e.iota(iotag_s[:], pattern=[[1, 512]], base=tc * 512,
                                             channel_multiplier=0, **IOK),
                     waits=[(sAC, sAC.n), (sDV, prev_gd_reader)])
            gde = S(lambda e, hh=hh: e.activation(gdec_s[:], iotag_s[:], AF.Exp,
                                                  scale=nslo_b[:, hh:hh + 1]),
                    waits=[(sGP, gio), (sDV, prev_gd_reader)])
            qh_ok = V(lambda e, tc=tc: e.tensor_mul(
                qt[:, tc * 512:(tc + 1) * 512], qraw[:, tc * 512:(tc + 1) * 512],
                gdec_s[:]), waits=[(sAC, gde), (sPE, scan_pe_done)])
            prev_gd_reader = qh_ok
            slot = 3 + tc % 2
            wws = [(sDV, qh_ok), (sDV, sinb_ok)]
            if slot in pfree:
                wws.append(pfree[slot])
            mm = T(lambda e, tc=tc, slot=slot: e.matmul(
                psF[slot][:, 0:512], s_in_bf[:], qt[:, tc * 512:(tc + 1) * 512],
                start=True, stop=True), waits=wws)
            cadd = V(lambda e, hh=hh, tc=tc, slot=slot: e.tensor_add(
                attn[:, hh * 2048 + tc * 512: hh * 2048 + (tc + 1) * 512],
                attn[:, hh * 2048 + tc * 512: hh * 2048 + (tc + 1) * 512],
                psF[slot][:, 0:512]),
                waits=[(sPE, mm), (sAC, sAC.n)])
            pfree[slot] = (sDV, cadd)
        prev_corr_pe = sPE.n
    barrier()
    if taps:
        emit("gpsimd", lambda e: e.dma_start(
            tap_attn.rearrange("(k p) t -> p k t", p=128),
            attn[:].rearrange("p (k t) -> p k t", t=2048)), inc=qDG, inc_n=16)
        emit("gpsimd", lambda e: e.dma_start(
            tap_q[0:128, :], qraw[:]), inc=qDG, inc_n=16)
        emit("gpsimd", lambda e: e.dma_start(
            tap_q[128:256, :], kT[:]), inc=qDG, inc_n=16)
        emit("gpsimd", lambda e: e.dma_start(
            tap_q[256:384, :], vT[:]), inc=qDG, inc_n=16)
        barrier()

    pfree = {}
    tmp_free = {}
    x1pre_marks = {}
    for half in range(2):
        wo_ok = None
        wws = [(qDG, woutbf_ok)] if half == 0 else [(qDG, woutbf_ok), (sPE, sPE.n)]
        for cc in range(16):
            wo_ok = Sd(lambda e, cc=cc, half=half: e.dma_start(
                woutW[:, cc * 1024:(cc + 1) * 1024],
                wout_bf[cc * 128:(cc + 1) * 128, half * 1024:(half + 1) * 1024]), waits=wws)
        for ml in range(8):
            mtile = half * 8 + ml
            for tc in range(4):
                slot = (ml * 4 + tc) % 4
                def mmw(e, ml=ml, tc=tc, slot=slot):
                    ins = None
                    for cc in range(16):
                        ins = e.matmul(
                            psF[slot][:, 0:512],
                            woutW[:, cc * 1024 + ml * 128: cc * 1024 + (ml + 1) * 128],
                            attn[:, cc * 2048 + tc * 512: cc * 2048 + tc * 512 + 512],
                            start=(cc == 0), stop=(cc == 15))
                    return ins
                wws2 = [(qDA, wo_ok)]
                if slot in pfree:
                    wws2.append(pfree[slot])
                mm = T(mmw, waits=wws2)
                tb = (mtile * 4 + tc) % 3
                wws3 = [(sPE, mm), (sDV, bo_ok)]
                if tb in tmp_free:
                    wws3.append(tmp_free[tb])
                ee = S(lambda e, tb=tb, slot=slot, mtile=mtile: e.activation(
                    tmp_s[tb][:], psF[slot][:, 0:512], AF.Identity,
                    bias=bo_cols[:, mtile:mtile + 1]), waits=wws3)
                pfree[slot] = (sAC, ee)
                ra = V(lambda e, mtile=mtile, tc=tc, tb=tb: e.tensor_add(
                    xT[:, mtile * 2048 + tc * 512: mtile * 2048 + tc * 512 + 512],
                    xT[:, mtile * 2048 + tc * 512: mtile * 2048 + tc * 512 + 512],
                    tmp_s[tb][:]), waits=[(sAC, ee)])
                tmp_free[tb] = (sDV, ra)
                x1pre_marks[(mtile, tc)] = ra
    x1_done = 0
    for tc in range(4):
        pre_ok = max(x1pre_marks[(m_, tc)] for m_ in range(16))
        def mm_sum(e, tc=tc):
            ins = None
            for cc in range(16):
                ins = e.matmul(psF[4][0:1, 0:512], onescol_b[:],
                               xT[:, cc * 2048 + tc * 512: cc * 2048 + tc * 512 + 512],
                               start=(cc == 0), stop=(cc == 15))
            return ins
        wws = [(sDV, pre_ok), (qDH, consts_ready)]
        if 4 in pfree:
            wws.append(pfree[4])
        mms = T(mm_sum, waits=wws)
        ev1 = V(lambda e: e.tensor_copy(r_sum[:], psF[4][0:1, 0:512]), waits=[(sPE, mms)])
        pfree[4] = (sDV, ev1)
        sq_marks = []
        for cc in range(16):
            tb = cc % 3
            wws = [(sDV, pre_ok)]
            if tb in tmp_free:
                wws.append(tmp_free[tb])
            if cc >= 3:
                wws.append((sPE, sq_marks[cc - 3]))
            sq = V(lambda e, cc=cc, tc=tc, tb=tb: e.tensor_mul(
                tmp_s[tb][:],
                xT[:, cc * 2048 + tc * 512: cc * 2048 + tc * 512 + 512],
                xT[:, cc * 2048 + tc * 512: cc * 2048 + tc * 512 + 512]), waits=wws)
            wws = [(sDV, sq)]
            if cc == 0 and 5 in pfree:
                wws.append(pfree[5])
            mq = T(lambda e, tb=tb, cc=cc: e.matmul(
                psF[5][0:1, 0:512], onescol_b[:], tmp_s[tb][:],
                start=(cc == 0), stop=(cc == 15)), waits=wws)
            tmp_free[tb] = (sPE, mq)
            sq_marks.append(mq)
        ev2 = V(lambda e: e.tensor_copy(r_sq[:], psF[5][0:1, 0:512]),
                waits=[(sPE, sq_marks[-1])])
        pfree[5] = (sDV, ev2)
        v1 = V(lambda e: e.tensor_scalar(r_mean[:], r_sum[:], 1.0 / 2048, None, ALU.mult),
               waits=[(sDV, ev1)])
        v2 = V(lambda e: e.tensor_scalar(r_sq[:], r_sq[:], 1.0 / 2048, None, ALU.mult),
               waits=[(sDV, ev2)])
        v3 = V(lambda e: e.tensor_mul(r_mrs[:], r_mean[:], r_mean[:]), waits=[(sDV, v1)])
        v4 = V(lambda e: e.tensor_sub(r_rstd[:], r_sq[:], r_mrs[:]),
               waits=[(sDV, v2), (sDV, v3)])
        v4b = V(lambda e: e.tensor_scalar(r_rstd[:], r_rstd[:], EPS, None, ALU.add),
                waits=[(sDV, v4)])
        a1 = S(lambda e: e.activation(r_rstd[:], r_rstd[:], AF.Ln),
               waits=[(sDV, v4b)])
        a2 = S(lambda e: e.activation(r_rstd[:], r_rstd[:], AF.Exp, scale=-0.5),
               waits=[(sAC, a1)])
        v5 = V(lambda e: e.tensor_mul(r_mrs[:], r_mean[:], r_rstd[:]), waits=[(sAC, a2)])
        wws = [(sAC, a2), pfree[4]]
        mb1 = T(lambda e: e.matmul(psF[4][:, 0:512], ones_f[:], r_rstd[:],
                                   start=True, stop=True), waits=wws)
        eb1 = V(lambda e: e.tensor_copy(rstd_b[:], psF[4][:, 0:512]), waits=[(sPE, mb1)])
        pfree[4] = (sDV, eb1)
        wws = [(sDV, v5), pfree[5]]
        mb2 = T(lambda e: e.matmul(psF[5][:, 0:512], ones_f[:], r_mrs[:],
                                   start=True, stop=True), waits=wws)
        eb2 = V(lambda e: e.tensor_copy(mrs_b[:], psF[5][:, 0:512]), waits=[(sPE, mb2)])
        pfree[5] = (sDV, eb2)
        for cc in range(16):
            n1 = V(lambda e, cc=cc, tc=tc: e.tensor_mul(
                xT[:, cc * 2048 + tc * 512: cc * 2048 + tc * 512 + 512],
                xT[:, cc * 2048 + tc * 512: cc * 2048 + tc * 512 + 512],
                rstd_b[:]), waits=[(sDV, eb1)])
            x1_done = V(lambda e, cc=cc, tc=tc: e.tensor_sub(
                xT[:, cc * 2048 + tc * 512: cc * 2048 + tc * 512 + 512],
                xT[:, cc * 2048 + tc * 512: cc * 2048 + tc * 512 + 512],
                mrs_b[:]), waits=[(sDV, n1), (sDV, eb2)])
    barrier()

    sp1 = Y(lambda e: e.dma_start(
        x1t_sp.rearrange("(k p) t -> p k t", p=128),
        xT[:].rearrange("p (k t) -> p k t", t=2048)))
    ptfree = {}
    stg_sent = [0, 0]
    x1k_ok = 0
    for ttile in range(16):
        sb_ = ttile % 2
        last_ee = 0
        for cc in range(16):
            tb = cc % 2
            wws = []
            if tb in ptfree:
                wws.append(ptfree[tb])
            if cc == 0 and stg_sent[sb_]:
                wws.append((sST, stg_sent[sb_]))
            mt_ = T(lambda e, cc=cc, ttile=ttile, tb=tb: e.transpose(
                psT[tb][:, 0:128],
                xT[:, cc * 2048 + ttile * 128: cc * 2048 + (ttile + 1) * 128],
                ident_b[:]), waits=wws)
            last_ee = S(lambda e, cc=cc, tb=tb, sb_=sb_: e.copy(
                x1kstg[sb_][:, cc * 128:(cc + 1) * 128], psT[tb][:, 0:128]),
                waits=[(sPE, mt_)])
            ptfree[tb] = (sAC, last_ee)
        snt = emit("sync", lambda e, ttile=ttile, sb_=sb_: e.dma_start(
            x1k_sp[ttile * 128:(ttile + 1) * 128, :], x1kstg[sb_][:]),
            waits=[(sAC, last_ee)], inc=sST, inc_n=16)
        stg_sent[sb_] = snt
        x1k_ok = snt

    gld = Y(lambda e: e.dma_start(gw_sb[:], gw_bfd), waits=[(qDG, gcast)])
    gwt_ok = 0
    for k in range(16):
        tb = k % 2
        wws = [(qDH, gld)]
        if tb in ptfree:
            wws.append(ptfree[tb])
        mt_ = T(lambda e, k=k, tb=tb: e.transpose(
            psT[tb][:, 0:32], gw_sb[0:32, k * 128:(k + 1) * 128], ident_b[0:32, 0:32]),
            waits=wws)
        gwt_ok = S(lambda e, k=k, tb=tb: e.copy(
            gwt[:, k * 32:(k + 1) * 32], psT[tb][:, 0:32]), waits=[(sPE, mt_)])
        ptfree[tb] = (sAC, gwt_ok)
    lg_ok = 0
    for ttile in range(16):
        slot = ttile % 4
        def mml(e, ttile=ttile, slot=slot):
            ins = None
            for cc in range(16):
                ins = e.matmul(psF[slot][:, 0:32],
                               xT[:, cc * 2048 + ttile * 128: cc * 2048 + (ttile + 1) * 128],
                               gwt[:, cc * 32:(cc + 1) * 32],
                               start=(cc == 0), stop=(cc == 15))
            return ins
        wws = [(sAC, gwt_ok), (sDV, x1_done)]
        if slot in pfree:
            wws.append(pfree[slot])
        mm = T(mml, waits=wws)
        lg_ok = S(lambda e, ttile=ttile, slot=slot: e.copy(
            probs[:, ttile * 32:(ttile + 1) * 32], psF[slot][:, 0:32]),
            waits=[(sPE, mm)])
        pfree[slot] = (sAC, lg_ok)
    NTL = 16
    mx_ok = 0
    for tt in range(NTL):
        mx_ok = V(lambda e, tt=tt: e.reduce_max(g_m[:, tt:tt + 1],
                                                probs[:, tt * 32:(tt + 1) * 32], axis=AX.X),
                  waits=[(sAC, lg_ok)])
    sub_ok = 0
    for tt in range(NTL):
        sub_ok = V(lambda e, tt=tt: e.tensor_scalar(
            probs[:, tt * 32:(tt + 1) * 32], probs[:, tt * 32:(tt + 1) * 32],
            g_m[:, tt:tt + 1], None, ALU.subtract), waits=[(sDV, mx_ok)])
    a0 = S(lambda e: e.activation(probs[:], probs[:], AF.Exp), waits=[(sDV, sub_ok)])
    sm_ok = 0
    for tt in range(NTL):
        sm_ok = V(lambda e, tt=tt: e.reduce_sum(g_m[:, tt:tt + 1],
                                                probs[:, tt * 32:(tt + 1) * 32], axis=AX.X),
                  waits=[(sAC, a0)])
    rc_ok = V(lambda e: e.reciprocal(g_run[:], g_m[:]), waits=[(sDV, sm_ok)])
    nm_ok = 0
    for tt in range(NTL):
        nm_ok = V(lambda e, tt=tt: e.tensor_scalar(
            probs[:, tt * 32:(tt + 1) * 32], probs[:, tt * 32:(tt + 1) * 32],
            g_run[:, tt:tt + 1], None, ALU.mult), waits=[(sDV, rc_ok)])
    w5 = V(lambda e: e.tensor_copy(probs0[:], probs[:]), waits=[(sDV, nm_ok)])
    wz = V(lambda e: e.memset(g_run[:], 0.0), waits=[(sDV, nm_ok)])
    last = max(w5, wz)
    for k8 in range(8):
        m0 = 0
        for tt in range(NTL):
            m0 = V(lambda e, tt=tt: e.reduce_max(g_m[:, tt:tt + 1],
                                                 probs[:, tt * 32:(tt + 1) * 32], axis=AX.X),
                   waits=[(sDV, last)])
        m1 = V(lambda e, k8=k8: e.tensor_copy(vs8[:, k8 * 16:(k8 + 1) * 16], g_m[:]),
               waits=[(sDV, m0)])
        m2 = V(lambda e: e.tensor_add(g_run[:], g_run[:], g_m[:]), waits=[(sDV, m0)])
        m3 = V(lambda e, k8=k8: e.tensor_copy(ps8[:, k8 * 16:(k8 + 1) * 16], g_run[:]),
               waits=[(sDV, m2)])
        last = max(m1, m3)
        if k8 < 7:
            eq_ok = 0
            for tt in range(NTL):
                eq_ok = V(lambda e, tt=tt: e.tensor_scalar(
                    g_eq[:, tt * 32:(tt + 1) * 32], probs[:, tt * 32:(tt + 1) * 32],
                    g_m[:, tt:tt + 1], None, ALU.is_equal), waits=[(sDV, m1)])
            m5 = V(lambda e: e.scalar_tensor_tensor(probs[:], g_eq[:], -1e30, probs[:],
                                                    ALU.mult, ALU.add), waits=[(sDV, eq_ok)])
            last = max(last, m5)
    t0 = V(lambda e: e.tensor_scalar(g_m[:], ps8[:, 7 * 16:8 * 16], 0.8, None, ALU.mult),
           waits=[(sDV, last)])
    tz = V(lambda e: e.memset(g_nk[:], 1.0), waits=[(sDV, last)])
    nk_ok = tz
    for k8 in range(7):
        c_ = V(lambda e, k8=k8: e.tensor_tensor(g_dd[:], ps8[:, k8 * 16:(k8 + 1) * 16],
                                                g_m[:], ALU.is_lt), waits=[(sDV, t0), (sDV, nk_ok)])
        nk_ok = V(lambda e: e.tensor_add(g_nk[:], g_nk[:], g_dd[:]), waits=[(sDV, c_)])
    vz = V(lambda e: e.memset(g_vst[:], 0.0), waits=[(sDV, last)])
    dz = V(lambda e: e.memset(g_dd[:], 0.0), waits=[(sDV, nk_ok)])
    vst_ok, dd_ok = vz, dz
    for k8 in range(8):
        oh = V(lambda e, k8=k8: e.tensor_scalar(g_t1[:, 0:16], g_nk[:], float(k8 + 1), None,
                                                ALU.is_equal), waits=[(sDV, nk_ok), (sDV, vst_ok), (sDV, dd_ok)])
        p1 = V(lambda e, k8=k8: e.tensor_mul(g_t1[:, 16:32], vs8[:, k8 * 16:(k8 + 1) * 16],
                                             g_t1[:, 0:16]), waits=[(sDV, oh)])
        vst_ok = V(lambda e: e.tensor_add(g_vst[:], g_vst[:], g_t1[:, 16:32]),
                   waits=[(sDV, p1)])
        p2 = V(lambda e, k8=k8: e.tensor_mul(g_t1[:, 32:48], ps8[:, k8 * 16:(k8 + 1) * 16],
                                             g_t1[:, 0:16]), waits=[(sDV, oh)])
        dd_ok = V(lambda e: e.tensor_add(g_dd[:], g_dd[:], g_t1[:, 32:48]),
                  waits=[(sDV, p2)])
    rd_ok = V(lambda e: e.reciprocal(g_nk[:], g_dd[:]), waits=[(sDV, dd_ok)])
    ge_ok = 0
    for tt in range(NTL):
        ge_ok = V(lambda e, tt=tt: e.tensor_scalar(
            g_eq[:, tt * 32:(tt + 1) * 32], probs0[:, tt * 32:(tt + 1) * 32],
            g_vst[:, tt:tt + 1], None, ALU.is_ge), waits=[(sDV, vst_ok), (sDV, w5)])
    tb_ = V(lambda e: e.tensor_mul(comb[:], probs0[:], g_eq[:]), waits=[(sDV, ge_ok)])
    comb_ok = 0
    for tt in range(NTL):
        comb_ok = V(lambda e, tt=tt: e.tensor_scalar(
            comb[:, tt * 32:(tt + 1) * 32], comb[:, tt * 32:(tt + 1) * 32],
            g_nk[:, tt:tt + 1], None, ALU.mult), waits=[(sDV, tb_), (sDV, rd_ok)])
    if taps:
        emit("gpsimd", lambda e: e.dma_start(
            tap_x1.rearrange("(k p) t -> p k t", p=128),
            xT[:].rearrange("p (k t) -> p k t", t=2048)),
            waits=[(sDV, x1_done)], inc=qDG, inc_n=16)
        emit("gpsimd", lambda e: e.dma_start(tap_comb, comb[:]),
             waits=[(sDV, comb_ok)], inc=qDG, inc_n=16)
    barrier()

    gu_done_pe = 0
    down_done_pe = 0
    x1w_users_pe = 0
    prev_pass_p7 = None
    out_sent_mark = 0
    x1kt_used = [0, 0]
    out_busy = [0, 0]
    pfree = {}
    wgut_saved = {}
    wdt_saved = {}
    cast_mark = {}
    tposed = {}
    g_done_pe = 0

    def emit_cast(e_, gate):
        Gd(lambda g, e_=e_: g.dma_start(wgu_bf[e_], wgu_d[e_]), waits=gate)
        cast_mark[e_] = Gd(lambda g, e_=e_: g.dma_start(wd_bf[e_], wd_d[e_]))

    for e_ in range(3):
        emit_cast(e_, [(sDV, x1_done)])
    for p_ in range(NPASS):
        wws = [(qDH, sp1)]
        if p_ > 0:
            wws.append((sPE, x1w_users_pe))
        x1w_ok = Y(lambda e, p_=p_: e.dma_start(
            x1w[:].rearrange("p (k t) -> p k t", t=TW),
            x1t_sp.rearrange("(k p) t -> p k t", p=128)[:, :, p_ * TW:(p_ + 1) * TW]),
            waits=wws)
        yz = None
        for m_ in range(TT1):
            wws = []
            if prev_pass_p7 is not None:
                wws = [prev_pass_p7]
            yz = Gp(lambda e, m_=m_: e.memset(yacc[:, m_ * 2048:(m_ + 1) * 2048], 0.0),
                    waits=wws)
        for e_ in range(E):
            if p_ == 0 and e_ + 3 < E:
                gate = ([(qDH, tposed[e_])] if e_ in tposed else [(qDH, qDH.n)])
                emit_cast(e_ + 3, gate)
            wwsA = [(qDG, cast_mark[e_]), (sPE, g_done_pe)]
            wwsB = [(qDG, cast_mark[e_]), (sPE, gu_done_pe)]
            if prev_pass_p7 is not None and e_ == 0:
                wwsA = wwsA + [(sST, out_sent_mark), prev_pass_p7]
            if p_ == 0:
                if e_ > 0:
                    wwsA.append((sST, wgut_saved[e_ - 1]))
                    wwsB.append((sST, wgut_saved[e_ - 1]))
                wgA_ev = None
                for k in range(16):
                    wgA_ev = Y(lambda s_, e_=e_, k=k: s_.dma_start(
                        wgut[:, k * GU:k * GU + 768],
                        wgu_bf[e_, 0:768, k * 128:(k + 1) * 128], transpose=True),
                        waits=wwsA)
                wgB_ev = None
                for k in range(16):
                    wgB_ev = Y(lambda s_, e_=e_, k=k: s_.dma_start(
                        wgut[:, k * GU + 768:(k + 1) * GU],
                        wgu_bf[e_, 768:1536, k * 128:(k + 1) * 128], transpose=True),
                        waits=wwsB)
                wg_ev = wgB_ev
            else:
                wg_ev = Y(lambda s_, e_=e_: s_.dma_start(
                    wgut[:], wgu_t_d[e_]), waits=wwsB + [(sST, wgut_saved[e_])])
                wgA_ev = wgB_ev = wg_ev
            wws = [(qDG, cast_mark[e_]), (sPE, down_done_pe)]
            wd_ev = None
            if p_ == 0:
                for k in range(6):
                    wd_ev = Y(lambda s_, e_=e_, k=k: s_.dma_start(
                        wdt[:, k * 2048:(k + 1) * 2048],
                        wd_bf[e_, :, k * 128:(k + 1) * 128], transpose=True), waits=wws)
                wgut_saved[e_] = emit("sync", lambda s_, e_=e_: s_.dma_start(
                    wgu_t_d[e_], wgut[:]), waits=[(qDH, wg_ev)], inc=sST, inc_n=16)
                wdt_saved[e_] = emit("sync", lambda s_, e_=e_: s_.dma_start(
                    wd_t_d[e_], wdt[:]), waits=[(qDH, wd_ev)], inc=sST, inc_n=16)
                tposed[e_] = qDH.n
            else:
                wd_ev = Y(lambda s_, e_=e_: s_.dma_start(
                    wdt[:], wd_t_d[e_]), waits=wws + [(sST, wdt_saved[e_])])
            gs_mark = {}
            h_ok = 0
            for iu in range(6):
                for tc in range(TC5):
                    slot = (iu * TC5 + tc) % 3
                    def mmg(t_, iu=iu, tc=tc, slot=slot):
                        ins = None
                        for cc in range(16):
                            ins = t_.matmul(
                                psF[slot][:, 0:512],
                                wgut[:, cc * GU + iu * 128: cc * GU + (iu + 1) * 128],
                                x1w[:, cc * TW + tc * 512: cc * TW + tc * 512 + 512],
                                start=(cc == 0), stop=(cc == 15))
                        return ins
                    wws = [(qDH, max(wgA_ev, x1w_ok))]
                    if slot in pfree:
                        wws.append(pfree[slot])
                    mmgm = T(mmg, waits=wws)
                    gs_mark[(iu, tc)] = S(lambda s_, iu=iu, tc=tc, slot=slot: s_.activation(
                        gsilf[:, iu * TW + tc * 512: iu * TW + tc * 512 + 512],
                        psF[slot][:, 0:512], AF.Silu), waits=[(sPE, mmgm)])
                    pfree[slot] = (sAC, gs_mark[(iu, tc)])
            g_done_pe = sPE.n
            for iu in range(6):
                for tc in range(TC5):
                    slot2 = 3 + (iu * TC5 + tc) % 3
                    def mmu(t_, iu=iu, tc=tc, slot2=slot2):
                        ins = None
                        for cc in range(16):
                            ins = t_.matmul(
                                psF[slot2][:, 0:512],
                                wgut[:, cc * GU + 768 + iu * 128: cc * GU + 768 + (iu + 1) * 128],
                                x1w[:, cc * TW + tc * 512: cc * TW + tc * 512 + 512],
                                start=(cc == 0), stop=(cc == 15))
                        return ins
                    wws = [(qDH, max(wgB_ev, x1w_ok))]
                    if slot2 in pfree:
                        wws.append(pfree[slot2])
                    mmum = T(mmu, waits=wws)
                    ue = S(lambda s_, iu=iu, tc=tc, slot2=slot2: s_.copy(
                        hbuf[:, iu * TW + tc * 512: iu * TW + tc * 512 + 512],
                        psF[slot2][:, 0:512]), waits=[(sPE, mmum)])
                    h_ok = V(lambda v_, iu=iu, tc=tc: v_.tensor_mul(
                        hbuf[:, iu * TW + tc * 512: iu * TW + tc * 512 + 512],
                        hbuf[:, iu * TW + tc * 512: iu * TW + tc * 512 + 512],
                        gsilf[:, iu * TW + tc * 512: iu * TW + tc * 512 + 512]),
                        waits=[(sAC, max(ue, gs_mark[(iu, tc)]))])
                    pfree[slot2] = (sDV, h_ok)
            gu_done_pe = sPE.n
            x1w_users_pe = sPE.n
            for ttile in range(TT1):
                for jc in range(4):
                    slot = 3 + (ttile * 4 + jc) % 3
                    def mmd(t_, ttile=ttile, jc=jc, slot=slot):
                        ins = None
                        for ic in range(6):
                            ins = t_.matmul(
                                psF[slot][:, 0:512],
                                hbuf[:, ic * TW + ttile * 128: ic * TW + (ttile + 1) * 128],
                                wdt[:, ic * 2048 + jc * 512: ic * 2048 + jc * 512 + 512],
                                start=(ic == 0), stop=(ic == 5))
                        return ins
                    wws = [(sDV, h_ok), (qDH, wd_ev)]
                    if slot in pfree:
                        wws.append(pfree[slot])
                    mmdm = T(mmd, waits=wws)
                    ttg = p_ * TT1 + ttile
                    ya = V(lambda v_, ttile=ttile, jc=jc, slot=slot, ttg=ttg, e_=e_:
                           v_.scalar_tensor_tensor(
                               yacc[:, ttile * 2048 + jc * 512: ttile * 2048 + jc * 512 + 512],
                               psF[slot][:, 0:512],
                               comb[:, ttg * 32 + e_: ttg * 32 + e_ + 1],
                               yacc[:, ttile * 2048 + jc * 512: ttile * 2048 + jc * 512 + 512],
                               ALU.mult, ALU.add),
                           waits=[(sPE, mmdm), (sGP, yz), (sDV, comb_ok)])
                    pfree[slot] = (sDV, ya)
            down_done_pe = sPE.n
        p7_last = 0
        for ttile in range(TT1):
            tb = ttile % 2
            wws = [(sST, x1k_ok), (sPE, gu_done_pe)]
            if x1kt_used[tb]:
                wws.append((sDV, x1kt_used[tb]))
            ld = Y(lambda e, p_=p_, ttile=ttile, tb=tb: e.dma_start(
                x1kt[tb][:], x1k_sp[p_ * TW + ttile * 128: p_ * TW + (ttile + 1) * 128, :]),
                waits=wws)
            yready = sDV.n
            c1 = V(lambda e, ttile=ttile, tb=tb: e.tensor_add(
                x2t[:], yacc[:, ttile * 2048:(ttile + 1) * 2048], x1kt[tb][:]),
                waits=[(qDH, ld), (sDV, yready)])
            x1kt_used[tb] = c1
            c2 = V(lambda e: e.reduce_sum(l_s1[:], x2t[:], axis=AX.X), waits=[(sDV, c1)])
            c3 = V(lambda e: e.tensor_mul(x2sq[:], x2t[:], x2t[:]), waits=[(sDV, c1)])
            c4 = V(lambda e: e.reduce_sum(l_s2[:], x2sq[:], axis=AX.X), waits=[(sDV, c3)])
            c5 = V(lambda e: e.tensor_scalar(l_mean[:], l_s1[:], 1.0 / 2048, None, ALU.mult),
                   waits=[(sDV, c2)])
            c6 = V(lambda e: e.tensor_mul(l_msq[:], l_mean[:], l_mean[:]), waits=[(sDV, c5)])
            c7 = V(lambda e: e.tensor_scalar(l_rstd[:], l_s2[:], 1.0 / 2048, None, ALU.mult),
                   waits=[(sDV, c4)])
            c8 = V(lambda e: e.tensor_sub(l_rstd[:], l_rstd[:], l_msq[:]),
                   waits=[(sDV, c7), (sDV, c6)])
            c8b = V(lambda e: e.tensor_scalar(l_rstd[:], l_rstd[:], EPS, None, ALU.add),
                    waits=[(sDV, c8)])
            c9 = S(lambda e: e.activation(l_rstd[:], l_rstd[:], AF.Ln),
                   waits=[(sDV, c8b)])
            ca = S(lambda e: e.activation(l_rstd[:], l_rstd[:], AF.Exp, scale=-0.5),
                   waits=[(sAC, c9)])
            ob = ttile % 2
            wws = [(sDV, c1), (sAC, ca)]
            if out_busy[ob]:
                wws.append((sST, out_busy[ob]))
            cb_ = V(lambda e, ob=ob: e.tensor_scalar(
                outstg[ob][:], x2t[:], l_mean[:, 0:1], l_rstd[:, 0:1],
                ALU.subtract, ALU.mult), waits=wws)
            snt = emit("sync", lambda e, p_=p_, ttile=ttile, ob=ob: e.dma_start(
                out_d[p_ * TW + ttile * 128: p_ * TW + (ttile + 1) * 128, :], outstg[ob][:]),
                waits=[(sDV, cb_)], inc=sST, inc_n=16)
            out_busy[ob] = snt
            out_sent_mark = snt
            p7_last = cb_
        prev_pass_p7 = (sDV, p7_last)

    emit_waits_only("sync", [(sST, sST.n)])

    block = ctx.enter_context(nc.Block())
    block.tensor(lambda t: [f(t) for f in ev["tensor"]])
    block.vector(lambda v: [f(v) for f in ev["vector"]])
    block.scalar(lambda s: [f(s) for f in ev["scalar"]])
    block.sync(lambda s: [f(s) for f in ev["sync"]])
    block.gpsimd(lambda g: [f(g) for f in ev["gpsimd"]])
    ctx.close()
    return nc


_NC_CACHE = None


def kernel(L0_attn_x, L0_attn_Wqkv, L0_attn_bqkv, L0_attn_Wout, L0_attn_bout,
           L0_attn_slopes, L0_ffn_gate_weight, L0_ffn_gate_up_proj,
           L0_ffn_down_proj):
    global _NC_CACHE
    x = np.ascontiguousarray(np.asarray(L0_attn_x, np.float32))
    wqkv = np.ascontiguousarray(np.asarray(L0_attn_Wqkv, np.float32))
    bqkv = np.ascontiguousarray(np.asarray(L0_attn_bqkv, np.float32))
    wout = np.ascontiguousarray(np.asarray(L0_attn_Wout, np.float32))
    bout = np.ascontiguousarray(np.asarray(L0_attn_bout, np.float32))
    slopes = np.ascontiguousarray(np.asarray(L0_attn_slopes, np.float32))
    gw = np.ascontiguousarray(np.asarray(L0_ffn_gate_weight, np.float32))
    wgu = np.ascontiguousarray(np.asarray(L0_ffn_gate_up_proj, np.float32))
    wd = np.ascontiguousarray(np.asarray(L0_ffn_down_proj, np.float32))

    if _NC_CACHE is None:
        _NC_CACHE = build_nc()
    nc = _NC_CACHE

    zeros = np.zeros((NT, C), np.float32)
    flag0 = np.zeros((128, 1), np.float32)
    flag1 = np.ones((128, 1), np.float32)
    in_maps = []
    for c in range(8):
        b, half = c // 2, c % 2
        x_mine = np.ascontiguousarray(x[b, half * NT:(half + 1) * NT, :])
        x_other = np.ascontiguousarray(x[b, 0:NT, :]) if half == 1 else zeros
        in_maps.append(dict(x_my=x_mine, x_ot=x_other, wqkv=wqkv, bqkv=bqkv,
                            wout=wout, bout=bout, slopes=slopes, gw=gw,
                            wgu=wgu, wd=wd, cflag=flag1 if half == 1 else flag0))
    res = run_bass_kernel_spmd(nc, in_maps, core_ids=list(range(8)))
    out = np.empty((B, N, C), np.float32)
    for c in range(8):
        b, half = c // 2, c % 2
        out[b, half * NT:(half + 1) * NT, :] = res.results[c]["out"]
    return out


# version 22 (final)
# speedup vs baseline: 1.4688x; 1.2887x over previous
import numpy as np
import ml_dtypes
import concourse.bass as bass
import concourse.mybir as mybir
from concourse.bass_utils import run_bass_kernel_spmd

F32 = mybir.dt.float32
BF16 = mybir.dt.bfloat16
AF = mybir.ActivationFunctionType
ALU = mybir.AluOpType
AX = mybir.AxisListType
BF = ml_dtypes.bfloat16

B, N, C = 4, 4096, 2048
H = 16
NT = 2048
E, GU, IH = 32, 1536, 768
TW = 1024
NPASS = NT // TW
TC5 = TW // 512
TT1 = TW // 128
EPS = 1e-5
PACE = False
ENGINES = ("tensor", "vector", "scalar", "sync", "gpsimd")


class Ctr:
    __slots__ = ("h", "n")
    def __init__(self, h):
        self.h = h
        self.n = 0
    def inc(self, k=1):
        self.n += k
        return self.n


def build_nc(taps=False):
    nc = bass.Bass("TRN2", target_bir_lowering=False)

    x_t = nc.dram_tensor("x_t", [C, NT], F32, kind="ExternalInput").ap()
    xo_t = nc.dram_tensor("xo_t", [C, NT], F32, kind="ExternalInput").ap()
    wkv_r = nc.dram_tensor("wkv_r", [8, 128, 16, 512], F32, kind="ExternalInput").ap()
    wqkv_h = nc.dram_tensor("wqkv_h", [16, 128, 16, 384], F32, kind="ExternalInput").ap()
    bqkv = nc.dram_tensor("bqkv", [3 * C], F32, kind="ExternalInput").ap()
    wout_h = nc.dram_tensor("wout_h", [2, 8, 128, 16, 128], F32, kind="ExternalInput").ap()
    bout = nc.dram_tensor("bout", [C], F32, kind="ExternalInput").ap()
    slopes = nc.dram_tensor("slopes", [H], F32, kind="ExternalInput").ap()
    cflag_d = nc.dram_tensor("cflag", [128, 1], F32, kind="ExternalInput").ap()
    gw_t = nc.dram_tensor("gw_t", [128, 16, E], F32, kind="ExternalInput").ap()
    wgu_p = nc.dram_tensor("wgu_p", [E, 2, 128, 16, 768], F32, kind="ExternalInput").ap()
    wd_p = nc.dram_tensor("wd_p", [E, 128, 6, 2048], F32, kind="ExternalInput").ap()
    out_d = nc.dram_tensor("out", [NT, C], F32, kind="ExternalOutput").ap()
    if taps:
        tap_attn = nc.dram_tensor("tap_attn", [16 * 128, NT], F32, kind="ExternalOutput").ap()
        tap_x1 = nc.dram_tensor("tap_x1", [16 * 128, NT], F32, kind="ExternalOutput").ap()
        tap_comb = nc.dram_tensor("tap_comb", [128, 16 * 32], F32, kind="ExternalOutput").ap()
        tap_sin = nc.dram_tensor("tap_sin", [128, 16 * 128], F32, kind="ExternalOutput").ap()
        tap_q = nc.dram_tensor("tap_q", [3 * 128, NT], F32, kind="ExternalOutput").ap()

    xmy_bf = nc.dram_tensor("xmy_bf", [C, NT], BF16, kind="Internal").ap()
    xot_bf = nc.dram_tensor("xot_bf", [4, 128, 16, 512], BF16, kind="Internal").ap()
    wkvr_bf = nc.dram_tensor("wkvr_bf", [8, 128, 16, 512], BF16, kind="Internal").ap()
    wqkvh_bf = nc.dram_tensor("wqkvh_bf", [16, 128, 16, 384], BF16, kind="Internal").ap()
    wouth_bf = nc.dram_tensor("wouth_bf", [2, 8, 128, 16, 128], BF16, kind="Internal").ap()
    gwt_bf = nc.dram_tensor("gwt_bf", [128, 16, E], BF16, kind="Internal").ap()
    wgu_bf = nc.dram_tensor("wgu_bf", [E, 2, 128, 16, 768], BF16, kind="Internal").ap()
    wd_bf = nc.dram_tensor("wd_bf", [E, 128, 6, 2048], BF16, kind="Internal").ap()
    x1t_sp = nc.dram_tensor("x1t_sp", [C, NT], BF16, kind="Internal").ap()
    x1k_sp = nc.dram_tensor("x1k_sp", [NT, C], BF16, kind="Internal").ap()

    ident_b_d = nc.inline_tensor(np.eye(128, dtype=BF), "ident_b_d").ap()
    ident_f_d = nc.inline_tensor(np.eye(128, dtype=np.float32), "ident_f_d").ap()
    ones_f_d = nc.inline_tensor(np.ones((1, 128), np.float32), "ones_f_d").ap()
    onescol_b_d = nc.inline_tensor(np.ones((128, 1), BF), "onescol_b_d").ap()

    def mk(name, shape, dt, at):
        return nc.alloc_sbuf_tensor_at(name, list(shape), dt, offset=at)

    def szof(shape, dt):
        nb = int(np.prod(shape[1:])) * (2 if dt == BF16 else 4)
        return (nb + 31) & ~31

    class Region:
        def __init__(self, base):
            self.o = base
        def al(self, name, shape, dt=F32):
            t = mk(name, shape, dt, self.o)
            self.o += szof(shape, dt)
            return t

    BASE0 = (nc.sbuf_base + 31) & ~31
    LIMIT = 224 * 1024
    P = Region(BASE0)
    ident_b = P.al("ident_b", [128, 128], BF16)
    ident_f = P.al("ident_f", [128, 128], F32)
    ones_f = P.al("ones_f", [1, 128])
    onescol_b = P.al("onescol_b", [128, 1], BF16)
    slo_row = P.al("slo_row", [1, 16])
    slo_b = P.al("slo_b", [128, 16])
    nslo_b = P.al("nslo_b", [128, 16])
    bd512_b = P.al("bd512_b", [128, 16])
    iotaW = P.al("iotaW", [128, 16])
    iotaP = P.al("iotaP", [128, 4])
    kdecR = P.al("kdecR", [128, 16 * 16])
    bq_pre = P.al("bq_pre", [48, 128])
    bq_cols = P.al("bq_cols", [128, 48])
    bo_pre = P.al("bo_pre", [16, 128])
    bo_cols = P.al("bo_cols", [128, 16])
    comb = P.al("comb", [128, 16 * 32])
    cflag = P.al("cflag_sb", [128, 1])
    l_s1 = P.al("l_s1", [128, 1])
    l_s2 = P.al("l_s2", [128, 1])
    l_s2b = P.al("l_s2b", [128, 1])
    l_mean = P.al("l_mean", [128, 1])
    l_msq = P.al("l_msq", [128, 1])
    l_rstd = P.al("l_rstd", [128, 1])
    OVL = P.o
    assert OVL <= BASE0 + 8 * 1024, OVL

    SIN = Region(OVL)
    s_in = SIN.al("s_in", [128, 16 * 128])
    BASE = SIN.o

    S7h = Region(OVL)
    x1kt = [S7h.al(f"x1kt{i}", [128, 2048], BF16) for i in range(2)]
    assert S7h.o <= BASE, S7h.o

    R = Region(BASE)
    xoT = [R.al(f"xoT{i}", [128, 16 * 512], BF16) for i in range(2)]
    wkvs = [R.al(f"wkvs{i}", [128, 16 * 512], BF16) for i in range(2)]
    kvf = R.al("kvf", [128, 32 * 512], BF16)
    ktokR = [R.al(f"ktokR{i}", [128, 4 * 128], BF16) for i in range(2)]
    vtokR = [R.al(f"vtokR{i}", [128, 4 * 128], BF16) for i in range(2)]
    assert R.o <= LIMIT

    A = Region(BASE)
    xT = A.al("xT", [128, 16 * 2048], BF16)
    attn = A.al("attn", [128, 16 * 2048], BF16)
    ATTN_TAIL = A.o
    qraw = A.al("qraw", [128, 2048], BF16)
    kT = A.al("kT", [128, 2048], BF16)
    vT = A.al("vT", [128, 2048], BF16)
    qkvW = [A.al(f"qkvW{i}", [128, 16 * 384], BF16) for i in range(2)]
    dmod = [A.al(f"dmod{i}", [128, 512]) for i in range(4)]
    masks = [A.al(f"masks{i}", [128, 512], BF16) for i in range(4)]
    iota512 = A.al("iota512", [128, 512])
    qd512 = A.al("qd512", [128, 512])
    qhat = [A.al(f"qhat{i}", [128, 512], BF16) for i in range(2)]
    scB = [A.al(f"scB{i}", [128, 512], BF16) for i in range(2)]
    ktokP = [A.al(f"ktokP{i}", [128, 4 * 128], BF16) for i in range(2)]
    vtokP = [A.al(f"vtokP{i}", [128, 4 * 128], BF16) for i in range(2)]
    kdecP = A.al("kdecP", [128, 4])
    s_f = A.al("s_f", [128, 128])
    s_bf = A.al("s_bf", [128, 128], BF16)
    assert A.o <= LIMIT, A.o
    assert R.o <= ATTN_TAIL, (R.o, ATTN_TAIL)

    Q = Region(ATTN_TAIL)
    woutW = Q.al("woutW", [128, 16 * 1024], BF16)
    tmp_s = [Q.al(f"tmp_s{i}", [128, 512], BF16) for i in range(3)]
    r_sum = Q.al("r_sum", [1, 512])
    r_sq = Q.al("r_sq", [1, 512])
    r_mean = Q.al("r_mean", [1, 512])
    r_rstd = Q.al("r_rstd", [1, 512])
    r_mrs = Q.al("r_mrs", [1, 512])
    rstd_b = Q.al("rstd_b", [128, 512])
    mrs_b = Q.al("mrs_b", [128, 512])
    x1kstg = [Q.al(f"x1kstg{i}", [128, 2048], BF16) for i in range(2)]
    assert Q.o <= LIMIT, Q.o

    G = Region(BASE + szof([128, 16 * 2048], BF16))
    gwt = G.al("gwt", [128, 16 * 32], BF16)
    probs = G.al("probs", [128, 16 * 32])
    probs0 = G.al("probs0", [128, 16 * 32])
    g_m = G.al("g_m", [128, 16])
    g_run = G.al("g_run", [128, 16])
    g_eq = G.al("g_eq", [128, 16 * 32])
    vs8 = G.al("vs8", [128, 16 * 8])
    ps8 = G.al("ps8", [128, 16 * 8])
    g_t1 = G.al("g_t1", [128, 16 * 32])
    g_nk = G.al("g_nk", [128, 16])
    g_vst = G.al("g_vst", [128, 16])
    g_dd = G.al("g_dd", [128, 16])

    M6 = Region(BASE)
    yacc = M6.al("yacc", [128, TT1 * 2048])
    wgut = M6.al("wgut", [128, 16 * GU], BF16)
    x1w = M6.al("x1w", [128, 16 * TW], BF16)
    wdt = M6.al("wdt", [128, 6 * 2048], BF16)
    hbuf = M6.al("hbuf", [128, 6 * TW], BF16)
    gsilf = M6.al("gsilf", [128, 6 * TW], BF16)
    assert M6.o <= LIMIT, M6.o

    psF = [nc.alloc_psum_tensor(f"pf{i}", [128, 512], F32) for i in range(6)]
    psT = [nc.alloc_psum_tensor(f"pt{i}", [128, 1024], BF16) for i in range(2)]

    from contextlib import ExitStack
    ctx = ExitStack()
    sem = lambda name: ctx.enter_context(nc.semaphore(name))
    qDG = Ctr(sem("qDG"))
    qDH = Ctr(sem("qDH"))
    sPE = Ctr(sem("sPE"))
    sDV = Ctr(sem("sDV"))
    sAC = Ctr(sem("sAC"))
    sGP = Ctr(sem("sGP"))
    sST = Ctr(sem("sST"))
    qDV = Ctr(sem("qDV"))

    ev = {k: [] for k in ENGINES}
    lastwait = {k: {} for k in ENGINES}

    def emit(eng, fn, waits=(), inc=None, inc_n=1):
        w = tuple(waits)
        def go(e, _fn=fn, _w=w, _i=inc, _n=inc_n, _eng=eng):
            for s, v in _w:
                lw = lastwait[_eng].get(id(s.h), -1)
                if v > lw:
                    e.wait_ge(s.h, v)
                    lastwait[_eng][id(s.h)] = v
            ins = _fn(e)
            if _i is not None:
                assert ins is not None
                ins.then_inc(_i.h, _n)
        ev[eng].append(go)
        if inc is not None:
            return inc.inc(inc_n)
        return None

    def emit_waits_only(eng, waits):
        w = tuple(waits)
        def go(e, _w=w, _eng=eng):
            for s, v in _w:
                lw = lastwait[_eng].get(id(s.h), -1)
                if v > lw:
                    e.wait_ge(s.h, v)
                    lastwait[_eng][id(s.h)] = v
        ev[eng].append(go)

    def barrier():
        marks = [(qDH, qDH.n), (sPE, sPE.n), (sDV, sDV.n),
                 (sAC, sAC.n), (sGP, sGP.n), (sST, sST.n)]
        for eng in ENGINES:
            emit_waits_only(eng, marks)

    T = lambda fn, waits=(), inc=sPE: emit("tensor", fn, waits=waits, inc=inc)
    V = lambda fn, waits=(), inc=sDV: emit("vector", fn, waits=waits, inc=inc)
    S = lambda fn, waits=(), inc=sAC: emit("scalar", fn, waits=waits, inc=inc)
    Y = lambda fn, waits=(), inc_n=16: emit("sync", fn, waits=waits, inc=qDH, inc_n=inc_n)
    Gp = lambda fn, waits=(), inc=sGP: emit("gpsimd", fn, waits=waits, inc=inc)
    Gd = lambda fn, waits=(): emit("gpsimd", fn, waits=waits, inc=qDG, inc_n=16)
    Vq = lambda fn, waits=(), inc=qDV, inc_n=16: emit("vector", fn, waits=waits, inc=inc, inc_n=inc_n)

    Y(lambda e: e.dma_start(ident_b[:], ident_b_d))
    Y(lambda e: e.dma_start(ident_f[:], ident_f_d))
    Y(lambda e: e.dma_start(ones_f[:], ones_f_d))
    Y(lambda e: e.dma_start(onescol_b[:], onescol_b_d))
    Y(lambda e: e.dma_start(slo_row[:], slopes.rearrange("(x a) -> x a", x=1)))
    Y(lambda e: e.dma_start(bq_pre[:], bqkv.rearrange("(a b) -> a b", b=128)))
    Y(lambda e: e.dma_start(bo_pre[:], bout.rearrange("(a b) -> a b", b=128)))
    consts_ready = Y(lambda e: e.dma_start(cflag[:], cflag_d))

    IOK = dict(allow_small_or_imprecise_dtypes=True)
    Gp(lambda e: e.iota(iotaW[:], pattern=[[-128, 16]], base=2048,
                        channel_multiplier=-1, **IOK))
    Gp(lambda e: e.iota(iotaP[:], pattern=[[-128, 4]], base=512,
                        channel_multiplier=-1, **IOK))
    Gp(lambda e: e.iota(iota512[:], pattern=[[1, 512]], base=0,
                        channel_multiplier=0, **IOK))
    io_done = 0
    for j in range(4):
        io_done = Gp(lambda e, j=j: e.iota(dmod[j][:], pattern=[[1, 512]],
                                           base=-128 * j, channel_multiplier=-1, **IOK))
    sin_z = None
    for hh in range(16):
        sin_z = Gp(lambda e, hh=hh: e.memset(s_in[:, hh * 128:(hh + 1) * 128], 0.0))

    xotw_ok = [0] * 4
    for w in range(4):
        xotw_ok[w] = Gd(lambda e, w=w: e.dma_start(
            xot_bf[w], xo_t.rearrange("(k p) t -> p k t", p=128)
            [:, :, w * 512:(w + 1) * 512]))
    wkvr_ok = [0] * 8
    for g in range(8):
        wkvr_ok[g] = Gd(lambda e, g=g: e.dma_start(wkvr_bf[g], wkv_r[g]))
    xmy_ok = Gd(lambda e: e.dma_start(xmy_bf, x_t))
    wqkvh_ok = [0] * 16
    for hh in range(16):
        wqkvh_ok[hh] = Gd(lambda e, hh=hh: e.dma_start(wqkvh_bf[hh], wqkv_h[hh]))
    wouth_ok = [0, 0]
    gwt_cast = 0
    castgu = [0] * E
    castwd = [0] * E
    moe_cast_q = list(range(E))

    def emit_moe_casts(n, gate_mark):
        for _ in range(n):
            if not moe_cast_q:
                return
            e_ = moe_cast_q.pop(0)
            Gd(lambda e, e_=e_: e.dma_start(wgu_bf[e_, 0], wgu_p[e_, 0]),
               waits=[(sPE, gate_mark)])
            castgu[e_] = Gd(lambda e, e_=e_: e.dma_start(wgu_bf[e_, 1], wgu_p[e_, 1]))
            castwd[e_] = Gd(lambda e, e_=e_: e.dma_start(wd_bf[e_], wd_p[e_]))

    if not PACE:
        for half in range(2):
            wouth_ok[half] = Gd(lambda e, half=half: e.dma_start(
                wouth_bf[half], wout_h[half]))
        gwt_cast = Gd(lambda e: e.dma_start(gwt_bf[:], gw_t))
        emit_moe_casts(E, 0)

    m = T(lambda e: e.matmul(psF[0][:, 0:16], ones_f[:], slo_row[:], start=True, stop=True),
          waits=[(qDH, consts_ready)])
    v = V(lambda e: e.tensor_copy(slo_b[:], psF[0][:, 0:16]), waits=[(sPE, m)])
    slo_ok = V(lambda e: e.tensor_scalar(nslo_b[:], slo_b[:], -1.0, None, ALU.mult),
               waits=[(sDV, v)])
    bd_ok = S(lambda e: e.activation(bd512_b[:], slo_b[:], AF.Exp, scale=-512.0),
              waits=[(sDV, slo_ok)])
    kd_ok = None
    for hh in range(16):
        kd_ok = S(lambda e, hh=hh: e.activation(
            kdecR[:, hh * 16:(hh + 1) * 16], iotaW[:], AF.Exp,
            scale=nslo_b[:, hh:hh + 1]),
            waits=[(sDV, slo_ok), (sGP, io_done)])
    dm_ok = 0
    for j in range(4):
        v = V(lambda e, j=j: e.tensor_scalar(qd512[:], dmod[j][:], 0.0, None, ALU.is_lt),
              waits=[(sGP, io_done), (sDV, dm_ok)])
        dm_ok = V(lambda e, j=j: e.scalar_tensor_tensor(
            dmod[j][:], qd512[:], 1e5, dmod[j][:], ALU.mult, ALU.add),
            waits=[(sDV, v)])
    m = T(lambda e: e.transpose(psF[1][:, 0:48], bq_pre[:], ident_f[0:48, 0:48]),
          waits=[(qDH, consts_ready)])
    bq_ok = V(lambda e: e.tensor_copy(bq_cols[:], psF[1][:, 0:48]), waits=[(sPE, m)])
    m = T(lambda e: e.transpose(psF[2][:, 0:16], bo_pre[:], ident_f[0:16, 0:16]),
          waits=[(qDH, consts_ready)])
    bo_ok = V(lambda e: e.tensor_copy(bo_cols[:], psF[2][:, 0:16]), waits=[(sPE, m)])
    barrier()

    NW = 4
    kv_mm_done = [0] * NW
    xoT_ready = [0, 0]
    wkv_free = {}
    pfree = {}
    ptfree = {}
    ktokR_free = {}
    for w in range(NW):
        wb = w % 2
        xw = [(qDG, xotw_ok[w])]
        if w >= 2:
            xw.append((sPE, kv_mm_done[w - 2]))
        xoT_ready[wb] = Y(lambda e, w=w, wb=wb: e.dma_start(
            xoT[wb][:].rearrange("p (k t) -> p k t", t=512),
            xot_bf[w]), waits=xw)
        if w >= 2:
            emit_moe_casts(1, kv_mm_done[w - 2])
        kv_last_ev = 0
        for g in range(8):
            gb = g % 2
            ww = [(qDG, wkvr_ok[g])]
            if gb in wkv_free:
                ww.append(wkv_free[gb])
            wk_ok = Y(lambda e, g=g, gb=gb: e.dma_start(
                wkvs[gb][:].rearrange("p (k m) -> p k m", m=512),
                wkvr_bf[g]), waits=ww)
            for mt in range(4):
                slot = (g * 4 + mt) % 4
                def mmkv(e, gb=gb, mt=mt, slot=slot, wb=wb):
                    ins = None
                    for cc in range(16):
                        ins = e.matmul(
                            psF[slot][:, 0:512],
                            wkvs[gb][:, cc * 512 + mt * 128: cc * 512 + (mt + 1) * 128],
                            xoT[wb][:, cc * 512:(cc + 1) * 512],
                            start=(cc == 0), stop=(cc == 15))
                    return ins
                ww2 = [(qDH, max(wk_ok, xoT_ready[wb]))]
                if slot in pfree:
                    ww2.append(pfree[slot])
                mm = T(mmkv, waits=ww2)
                col = 16 + 4 * g + mt
                ek = S(lambda e, g=g, mt=mt, slot=slot, col=col: e.activation(
                    kvf[:, (4 * g + mt) * 512: (4 * g + mt + 1) * 512],
                    psF[slot][:, 0:512], AF.Identity, bias=bq_cols[:, col:col + 1]),
                    waits=[(sPE, mm), (sDV, bq_ok)])
                pfree[slot] = (sAC, ek)
                kv_last_ev = ek
            wkv_free[gb] = (sPE, sPE.n)
        kv_mm_done[w] = sPE.n
        for hh in range(16):
            hb2 = hh % 2
            tr_ev = {}
            for j in range(4):
                for which in range(2):
                    srccol = hh if which == 0 else 16 + hh
                    tb = (j * 2 + which) % 2
                    wws = [(sAC, kv_last_ev)]
                    if tb in ptfree:
                        wws.append(ptfree[tb])
                    mt_ = T(lambda e, srccol=srccol, j=j, tb=tb: e.transpose(
                        psT[tb][:, 0:128],
                        kvf[:, srccol * 512 + j * 128: srccol * 512 + (j + 1) * 128],
                        ident_b[:]), waits=wws)
                    if which == 0:
                        col = w * 4 + j
                        wwk = [(sPE, mt_), (sAC, kd_ok)]
                        if hb2 in ktokR_free:
                            wwk.append(ktokR_free[hb2])
                        ee = V(lambda e, j=j, tb=tb, hh=hh, col=col, hb2=hb2:
                               e.tensor_scalar(
                                   ktokR[hb2][:, j * 128:(j + 1) * 128],
                                   psT[tb][:, 0:128],
                                   kdecR[:, hh * 16 + col: hh * 16 + col + 1],
                                   None, ALU.mult), waits=wwk)
                        ptfree[tb] = (sDV, ee)
                        tr_ev[(j, 0)] = (sDV, ee)
                    else:
                        wwv = [(sPE, mt_)]
                        if hb2 in ktokR_free:
                            wwv.append(ktokR_free[hb2])
                        ee = S(lambda e, j=j, tb=tb, hb2=hb2: e.copy(
                            vtokR[hb2][:, j * 128:(j + 1) * 128], psT[tb][:, 0:128]),
                            waits=wwv)
                        ptfree[tb] = (sAC, ee)
                        tr_ev[(j, 1)] = (sAC, ee)
            slot = 4 + hb2
            wws = [tr_ev[(j, x)] for j in range(4) for x in range(2)]
            if slot in pfree:
                wws.append(pfree[slot])
            def mmS(e, hb2=hb2, slot=slot):
                ins = None
                for j in range(4):
                    ins = e.matmul(
                        psF[slot][:, 0:128],
                        ktokR[hb2][:, j * 128:(j + 1) * 128],
                        vtokR[hb2][:, j * 128:(j + 1) * 128],
                        start=(j == 0), stop=(j == 3))
                return ins
            mm = T(mmS, waits=wws)
            ktokR_free[hb2] = (sPE, mm)
            st_ = V(lambda e, hh=hh, slot=slot: e.tensor_add(
                s_in[:, hh * 128:(hh + 1) * 128], s_in[:, hh * 128:(hh + 1) * 128],
                psF[slot][:, 0:128]),
                waits=[(sPE, mm), (sGP, sin_z)])
            pfree[slot] = (sDV, st_)
    barrier()

    if taps:
        emit("gpsimd", lambda e: e.dma_start(
            tap_sin.rearrange("p (k t) -> p k t", t=128),
            s_in[:].rearrange("p (k t) -> p k t", t=128)), inc=qDG, inc_n=16)

    qkv_pre = {}
    for h0 in range(2):
        qkv_pre[h0] = Y(lambda e, h0=h0: e.dma_start(
            qkvW[h0][:].rearrange("p (k m) -> p k m", m=384),
            wqkvh_bf[h0]), waits=[(qDG, wqkvh_ok[h0])])
    xt_ready = 0
    for k in range(16):
        xt_ready = Y(lambda e, k=k: e.dma_start(
            xT[:, k * 2048:(k + 1) * 2048],
            xmy_bf[k * 128:(k + 1) * 128, :]),
            waits=[(qDG, xmy_ok),
                   (sPE, kv_mm_done[2] if k < 4 else kv_mm_done[3])])

    pfree = {}
    ptfree = {}
    qkvW_free = {}
    prev_mask_rd = [0] * 4
    prev_qd_rd = 0
    prev_kp_rd = 0
    prev_qraw_dve = 0
    prev_sbf_pe = 0
    kp_free = {}
    scB_free = [None, None]
    qhat_free = [None, None]
    for hh in range(16):
        hb = hh % 2
        emit_moe_casts(2, sPE.n)
        if PACE and hh == 5:
            wouth_ok[0] = Gd(lambda e: e.dma_start(wouth_bf[0], wout_h[0]),
                             waits=[(sPE, sPE.n)])
        if PACE and hh == 6:
            wouth_ok[1] = Gd(lambda e: e.dma_start(wouth_bf[1], wout_h[1]),
                             waits=[(sPE, sPE.n)])
        if PACE and hh == 7:
            gwt_cast = Gd(lambda e: e.dma_start(gwt_bf[:], gw_t),
                          waits=[(sPE, sPE.n)])
        if hh < 2:
            wq_ok = qkv_pre[hh]
        else:
            ww = [(qDG, wqkvh_ok[hh])]
            if hb in qkvW_free:
                ww.append(qkvW_free[hb])
            wq_ok = Y(lambda e, hh=hh, hb=hb: e.dma_start(
                qkvW[hb][:].rearrange("p (k m) -> p k m", m=384),
                wqkvh_bf[hh]), waits=ww)
        ev_done = {}
        first_ek = True
        for part, dstbuf in ((0, qraw), (1, kT), (2, vT)):
            for tc in range(4):
                slot = (part * 4 + tc) % 2
                def mmq(e, hb=hb, part=part, tc=tc, slot=slot):
                    ins = None
                    for cc in range(16):
                        ins = e.matmul(
                            psF[slot][:, 0:512],
                            qkvW[hb][:, cc * 384 + part * 128: cc * 384 + (part + 1) * 128],
                            xT[:, cc * 2048 + tc * 512: cc * 2048 + tc * 512 + 512],
                            start=(cc == 0), stop=(cc == 15))
                    return ins
                wws = [(qDH, max(wq_ok, xt_ready))]
                if slot in pfree:
                    wws.append(pfree[slot])
                mm = T(mmq, waits=wws)
                col = part * 16 + hh
                wws3 = [(sPE, mm), (sDV, bq_ok)]
                if first_ek:
                    wws3.append((sDV, prev_qraw_dve))
                    first_ek = False
                ee = S(lambda e, dstbuf=dstbuf, tc=tc, slot=slot, col=col:
                       e.activation(dstbuf[:, tc * 512:(tc + 1) * 512], psF[slot][:, 0:512],
                                    AF.Identity, bias=bq_cols[:, col:col + 1]),
                       waits=wws3)
                pfree[slot] = (sAC, ee)
                ev_done[(part, tc)] = ee
        qkvW_free[hb] = (sPE, sPE.n)
        q_ok = max(ev_done[(0, tc)] for tc in range(4))
        k_ok = max(ev_done[(1, tc)] for tc in range(4))
        v_ok = max(ev_done[(2, tc)] for tc in range(4))
        mk_ev = []
        for j in range(4):
            mk_ev.append(S(lambda e, j=j, hh=hh: e.activation(
                masks[j][:], dmod[j][:], AF.Exp, scale=nslo_b[:, hh:hh + 1]),
                waits=[(sDV, max(dm_ok, slo_ok, prev_mask_rd[j]))]))
        qd_ev = S(lambda e, hh=hh: e.activation(
            qd512[:], iota512[:], AF.Exp, scale=nslo_b[:, hh:hh + 1]),
            waits=[(sDV, max(slo_ok, prev_qd_rd)), (sGP, io_done)])
        kp_ev = S(lambda e, hh=hh: e.activation(
            kdecP[:], iotaP[:], AF.Exp, scale=nslo_b[:, hh:hh + 1]),
            waits=[(sDV, max(slo_ok, prev_kp_rd)), (sGP, io_done)])
        si = V(lambda e, hh=hh: e.tensor_scalar(
            s_f[:], s_in[:, hh * 128:(hh + 1) * 128], cflag[:, 0:1], None, ALU.mult),
            waits=[(sPE, prev_sbf_pe), (qDH, consts_ready)])
        sb_mark = V(lambda e: e.tensor_copy(s_bf[:], s_f[:]), waits=[(sDV, si)])
        for w in range(4):
            wb2 = w % 2
            wwq = [(sAC, max(q_ok, qd_ev))]
            if qhat_free[wb2] is not None:
                wwq.append(qhat_free[wb2])
            qh_ev = V(lambda e, w=w, wb2=wb2: e.tensor_mul(
                qhat[wb2][:], qraw[:, w * 512:(w + 1) * 512], qd512[:]), waits=wwq)
            prev_qd_rd = qh_ev
            prev_qraw_dve = qh_ev
            tr_ev = {}

            def do_tr(j, w=w, wb2=wb2, tr_ev=tr_ev, kp_ev=kp_ev,
                      k_ok=k_ok, v_ok=v_ok):
                for which in range(2):
                    tb = (j * 2 + which) % 2
                    srcb = kT if which == 0 else vT
                    wws = [(sAC, k_ok if which == 0 else v_ok)]
                    if tb in ptfree:
                        wws.append(ptfree[tb])
                    mt_ = T(lambda e, srcb=srcb, j=j, tb=tb, w=w: e.transpose(
                        psT[tb][:, 0:128],
                        srcb[:, w * 512 + j * 128: w * 512 + (j + 1) * 128],
                        ident_b[:]), waits=wws)
                    if which == 0:
                        wwk = [(sPE, mt_), (sAC, kp_ev)]
                        if wb2 in kp_free:
                            wwk.append(kp_free[wb2])
                        ee = V(lambda e, j=j, tb=tb, wb2=wb2: e.tensor_scalar(
                            ktokP[wb2][:, j * 128:(j + 1) * 128], psT[tb][:, 0:128],
                            kdecP[:, j:j + 1], None, ALU.mult), waits=wwk)
                        ptfree[tb] = (sDV, ee)
                        tr_ev[(j, 0)] = (sDV, ee)
                    else:
                        wwv = [(sPE, mt_)]
                        if wb2 in kp_free:
                            wwv.append(kp_free[wb2])
                        ee = S(lambda e, j=j, tb=tb, wb2=wb2: e.copy(
                            vtokP[wb2][:, j * 128:(j + 1) * 128], psT[tb][:, 0:128]),
                            waits=wwv)
                        ptfree[tb] = (sAC, ee)
                        tr_ev[(j, 1)] = (sAC, ee)

            do_tr(0)
            do_tr(1)
            oj_mark = 0
            for j in range(4):
                sslot = 2 + j % 2
                wws = [(sAC, max(k_ok, q_ok))]
                if sslot in pfree:
                    wws.append(pfree[sslot])
                mm_s = T(lambda e, j=j, w=w, sslot=sslot: e.matmul(
                    psF[sslot][:, 0:512],
                    kT[:, w * 512 + j * 128: w * 512 + (j + 1) * 128],
                    qraw[:, w * 512:(w + 1) * 512],
                    start=True, stop=True), waits=wws)
                wwsc = [(sPE, mm_s), (sAC, mk_ev[j])]
                if scB_free[j % 2] is not None:
                    wwsc.append(scB_free[j % 2])
                sc = V(lambda e, j=j, sslot=sslot: e.tensor_mul(
                    scB[j % 2][:], psF[sslot][:, 0:512], masks[j][:]), waits=wwsc)
                pfree[sslot] = (sDV, sc)
                prev_mask_rd[j] = sc
                if j >= 2:
                    pass
                else:
                    do_tr(j + 2)
                wws = [(sDV, sc), tr_ev[(j, 1)]]
                if j == 0 and 4 in pfree:
                    wws.append(pfree[4])
                oj_mark = T(lambda e, j=j, wb2=wb2: e.matmul(
                    psF[4][:, 0:512], vtokP[wb2][:, j * 128:(j + 1) * 128],
                    scB[j % 2][:], start=(j == 0), stop=(j == 3)), waits=wws)
                scB_free[j % 2] = (sPE, oj_mark)
                if j == 2:
                    o_int = T(lambda e, wb2=wb2: e.matmul(
                        psF[4][:, 0:512], s_bf[:], qhat[wb2][:],
                        start=False, stop=False),
                        waits=[(sDV, sb_mark), (sDV, qh_ev)])
                    qhat_free[wb2] = (sPE, o_int)
                    prev_sbf_pe = o_int
            ev_o = S(lambda e, hh=hh, w=w: e.copy(
                attn[:, hh * 2048 + w * 512: hh * 2048 + (w + 1) * 512],
                psF[4][:, 0:512]), waits=[(sPE, oj_mark)])
            pfree[4] = (sAC, ev_o)
            wws = [tr_ev[(j, x)] for j in range(4) for x in range(2)]
            if 5 in pfree:
                wws.append(pfree[5])
            def mmU(e, wb2=wb2):
                ins = None
                for j in range(4):
                    ins = e.matmul(
                        psF[5][:, 0:128],
                        ktokP[wb2][:, j * 128:(j + 1) * 128],
                        vtokP[wb2][:, j * 128:(j + 1) * 128],
                        start=(j == 0), stop=(j == 3))
                return ins
            mmu_ = T(mmU, waits=wws)
            kp_free[wb2] = (sPE, mmu_)
            prev_kp_rd = sDV.n
            st_ = V(lambda e, hh=hh: e.scalar_tensor_tensor(
                s_f[:], s_f[:], bd512_b[:, hh:hh + 1], psF[5][:, 0:128],
                ALU.mult, ALU.add),
                waits=[(sPE, mmu_), (sAC, bd_ok)])
            pfree[5] = (sDV, st_)
            if w < 3:
                sb_mark = V(lambda e: e.tensor_copy(s_bf[:], s_f[:]),
                            waits=[(sDV, st_), (sPE, prev_sbf_pe)])
    barrier()
    if taps:
        emit("gpsimd", lambda e: e.dma_start(
            tap_attn.rearrange("(k p) t -> p k t", p=128),
            attn[:].rearrange("p (k t) -> p k t", t=2048)), inc=qDG, inc_n=16)
        emit("gpsimd", lambda e: e.dma_start(
            tap_q[0:128, :], qraw[:]), inc=qDG, inc_n=16)
        emit("gpsimd", lambda e: e.dma_start(
            tap_q[128:256, :], kT[:]), inc=qDG, inc_n=16)
        emit("gpsimd", lambda e: e.dma_start(
            tap_q[256:384, :], vT[:]), inc=qDG, inc_n=16)
        barrier()

    pfree = {}
    tmp_free = {}
    x1pre_marks = {}
    for half in range(2):
        wws = [(qDG, wouth_ok[half])] if half == 0 else \
              [(qDG, wouth_ok[half]), (sPE, sPE.n)]
        wo_ml = [0] * 8
        for ml in range(8):
            wo_ml[ml] = Y(lambda e, ml=ml, half=half: e.dma_start(
                woutW[:, ml * 2048:(ml + 1) * 2048],
                wouth_bf[half, ml].rearrange("p k m -> p (k m)")), waits=wws)
        for ml in range(8):
            mtile = half * 8 + ml
            for tc in range(4):
                slot = (ml * 4 + tc) % 4
                def mmw(e, ml=ml, tc=tc, slot=slot):
                    ins = None
                    for cc in range(16):
                        ins = e.matmul(
                            psF[slot][:, 0:512],
                            woutW[:, ml * 2048 + cc * 128: ml * 2048 + (cc + 1) * 128],
                            attn[:, cc * 2048 + tc * 512: cc * 2048 + tc * 512 + 512],
                            start=(cc == 0), stop=(cc == 15))
                    return ins
                wws2 = [(qDH, wo_ml[ml])]
                if slot in pfree:
                    wws2.append(pfree[slot])
                mm = T(mmw, waits=wws2)
                tb = (mtile * 4 + tc) % 3
                wws3 = [(sPE, mm), (sDV, bo_ok)]
                if tb in tmp_free:
                    wws3.append(tmp_free[tb])
                ee = S(lambda e, tb=tb, slot=slot, mtile=mtile: e.activation(
                    tmp_s[tb][:], psF[slot][:, 0:512], AF.Identity,
                    bias=bo_cols[:, mtile:mtile + 1]), waits=wws3)
                pfree[slot] = (sAC, ee)
                ra = V(lambda e, mtile=mtile, tc=tc, tb=tb: e.tensor_add(
                    xT[:, mtile * 2048 + tc * 512: mtile * 2048 + tc * 512 + 512],
                    xT[:, mtile * 2048 + tc * 512: mtile * 2048 + tc * 512 + 512],
                    tmp_s[tb][:]), waits=[(sAC, ee)])
                tmp_free[tb] = (sDV, ra)
                x1pre_marks[(mtile, tc)] = ra
    x1_done = 0
    for tc in range(4):
        pre_ok = max(x1pre_marks[(m_, tc)] for m_ in range(16))
        def mm_sum(e, tc=tc):
            ins = None
            for cc in range(16):
                ins = e.matmul(psF[4][0:1, 0:512], onescol_b[:],
                               xT[:, cc * 2048 + tc * 512: cc * 2048 + tc * 512 + 512],
                               start=(cc == 0), stop=(cc == 15))
            return ins
        wws = [(sDV, pre_ok), (qDH, consts_ready)]
        if 4 in pfree:
            wws.append(pfree[4])
        mms = T(mm_sum, waits=wws)
        ev1 = V(lambda e: e.tensor_copy(r_sum[:], psF[4][0:1, 0:512]), waits=[(sPE, mms)])
        pfree[4] = (sDV, ev1)
        sq_marks = []
        for cc in range(16):
            tb = cc % 3
            wws = [(sDV, pre_ok)]
            if tb in tmp_free:
                wws.append(tmp_free[tb])
            if cc >= 3:
                wws.append((sPE, sq_marks[cc - 3]))
            sq = V(lambda e, cc=cc, tc=tc, tb=tb: e.tensor_mul(
                tmp_s[tb][:],
                xT[:, cc * 2048 + tc * 512: cc * 2048 + tc * 512 + 512],
                xT[:, cc * 2048 + tc * 512: cc * 2048 + tc * 512 + 512]), waits=wws)
            wws = [(sDV, sq)]
            if cc == 0 and 5 in pfree:
                wws.append(pfree[5])
            mq = T(lambda e, tb=tb, cc=cc: e.matmul(
                psF[5][0:1, 0:512], onescol_b[:], tmp_s[tb][:],
                start=(cc == 0), stop=(cc == 15)), waits=wws)
            tmp_free[tb] = (sPE, mq)
            sq_marks.append(mq)
        ev2 = V(lambda e: e.tensor_copy(r_sq[:], psF[5][0:1, 0:512]),
                waits=[(sPE, sq_marks[-1])])
        pfree[5] = (sDV, ev2)
        v1 = V(lambda e: e.tensor_scalar(r_mean[:], r_sum[:], 1.0 / 2048, None, ALU.mult),
               waits=[(sDV, ev1)])
        v2 = V(lambda e: e.tensor_scalar(r_sq[:], r_sq[:], 1.0 / 2048, None, ALU.mult),
               waits=[(sDV, ev2)])
        v3 = V(lambda e: e.tensor_mul(r_mrs[:], r_mean[:], r_mean[:]), waits=[(sDV, v1)])
        v4 = V(lambda e: e.tensor_sub(r_rstd[:], r_sq[:], r_mrs[:]),
               waits=[(sDV, v2), (sDV, v3)])
        v4b = V(lambda e: e.tensor_scalar(r_rstd[:], r_rstd[:], EPS, None, ALU.add),
                waits=[(sDV, v4)])
        a1 = S(lambda e: e.activation(r_rstd[:], r_rstd[:], AF.Ln),
               waits=[(sDV, v4b)])
        a2 = S(lambda e: e.activation(r_rstd[:], r_rstd[:], AF.Exp, scale=-0.5),
               waits=[(sAC, a1)])
        v5 = V(lambda e: e.tensor_mul(r_mrs[:], r_mean[:], r_rstd[:]), waits=[(sAC, a2)])
        wws = [(sAC, a2), pfree[4]]
        mb1 = T(lambda e: e.matmul(psF[4][:, 0:512], ones_f[:], r_rstd[:],
                                   start=True, stop=True), waits=wws)
        eb1 = V(lambda e: e.tensor_copy(rstd_b[:], psF[4][:, 0:512]), waits=[(sPE, mb1)])
        pfree[4] = (sDV, eb1)
        wws = [(sDV, v5), pfree[5]]
        mb2 = T(lambda e: e.matmul(psF[5][:, 0:512], ones_f[:], r_mrs[:],
                                   start=True, stop=True), waits=wws)
        eb2 = V(lambda e: e.tensor_copy(mrs_b[:], psF[5][:, 0:512]), waits=[(sPE, mb2)])
        pfree[5] = (sDV, eb2)
        for cc in range(16):
            n1 = V(lambda e, cc=cc, tc=tc: e.tensor_mul(
                xT[:, cc * 2048 + tc * 512: cc * 2048 + tc * 512 + 512],
                xT[:, cc * 2048 + tc * 512: cc * 2048 + tc * 512 + 512],
                rstd_b[:]), waits=[(sDV, eb1)])
            x1_done = V(lambda e, cc=cc, tc=tc: e.tensor_sub(
                xT[:, cc * 2048 + tc * 512: cc * 2048 + tc * 512 + 512],
                xT[:, cc * 2048 + tc * 512: cc * 2048 + tc * 512 + 512],
                mrs_b[:]), waits=[(sDV, n1), (sDV, eb2)])
    barrier()

    sp1 = Y(lambda e: e.dma_start(
        x1t_sp.rearrange("(k p) t -> p k t", p=128),
        xT[:].rearrange("p (k t) -> p k t", t=2048)))
    ptfree = {}
    stg_sent = [0, 0]
    x1k_ok = 0
    for ttile in range(16):
        sb_ = ttile % 2
        last_ee = 0
        for cc in range(16):
            tb = cc % 2
            wws = []
            if tb in ptfree:
                wws.append(ptfree[tb])
            if cc == 0 and stg_sent[sb_]:
                wws.append((sST, stg_sent[sb_]))
            mt_ = T(lambda e, cc=cc, ttile=ttile, tb=tb: e.transpose(
                psT[tb][:, 0:128],
                xT[:, cc * 2048 + ttile * 128: cc * 2048 + (ttile + 1) * 128],
                ident_b[:]), waits=wws)
            last_ee = S(lambda e, cc=cc, tb=tb, sb_=sb_: e.copy(
                x1kstg[sb_][:, cc * 128:(cc + 1) * 128], psT[tb][:, 0:128]),
                waits=[(sPE, mt_)])
            ptfree[tb] = (sAC, last_ee)
        snt = emit("sync", lambda e, ttile=ttile, sb_=sb_: e.dma_start(
            x1k_sp[ttile * 128:(ttile + 1) * 128, :], x1kstg[sb_][:]),
            waits=[(sAC, last_ee)], inc=sST, inc_n=16)
        stg_sent[sb_] = snt
        x1k_ok = snt

    gwt_ld = Y(lambda e: e.dma_start(
        gwt[:].rearrange("p (k j) -> p k j", j=32), gwt_bf),
        waits=[(qDG, gwt_cast)])
    lg_ok = 0
    for ttile in range(16):
        slot = ttile % 4
        def mml(e, ttile=ttile, slot=slot):
            ins = None
            for cc in range(16):
                ins = e.matmul(psF[slot][:, 0:32],
                               xT[:, cc * 2048 + ttile * 128: cc * 2048 + (ttile + 1) * 128],
                               gwt[:, cc * 32:(cc + 1) * 32],
                               start=(cc == 0), stop=(cc == 15))
            return ins
        wws = [(qDH, gwt_ld), (sDV, x1_done)]
        if slot in pfree:
            wws.append(pfree[slot])
        mm = T(mml, waits=wws)
        lg_ok = S(lambda e, ttile=ttile, slot=slot: e.copy(
            probs[:, ttile * 32:(ttile + 1) * 32], psF[slot][:, 0:32]),
            waits=[(sPE, mm)])
        pfree[slot] = (sAC, lg_ok)
    NTL = 16
    mx_ok = 0
    for tt in range(NTL):
        mx_ok = V(lambda e, tt=tt: e.reduce_max(g_m[:, tt:tt + 1],
                                                probs[:, tt * 32:(tt + 1) * 32], axis=AX.X),
                  waits=[(sAC, lg_ok)])
    sub_ok = 0
    for tt in range(NTL):
        sub_ok = V(lambda e, tt=tt: e.tensor_scalar(
            probs[:, tt * 32:(tt + 1) * 32], probs[:, tt * 32:(tt + 1) * 32],
            g_m[:, tt:tt + 1], None, ALU.subtract), waits=[(sDV, mx_ok)])
    a0 = S(lambda e: e.activation(probs[:], probs[:], AF.Exp), waits=[(sDV, sub_ok)])
    sm_ok = 0
    for tt in range(NTL):
        sm_ok = V(lambda e, tt=tt: e.reduce_sum(g_m[:, tt:tt + 1],
                                                probs[:, tt * 32:(tt + 1) * 32], axis=AX.X),
                  waits=[(sAC, a0)])
    rc_ok = V(lambda e: e.reciprocal(g_run[:], g_m[:]), waits=[(sDV, sm_ok)])
    nm_ok = 0
    for tt in range(NTL):
        nm_ok = V(lambda e, tt=tt: e.tensor_scalar(
            probs[:, tt * 32:(tt + 1) * 32], probs[:, tt * 32:(tt + 1) * 32],
            g_run[:, tt:tt + 1], None, ALU.mult), waits=[(sDV, rc_ok)])
    w5 = V(lambda e: e.tensor_copy(probs0[:], probs[:]), waits=[(sDV, nm_ok)])
    wz = V(lambda e: e.memset(g_run[:], 0.0), waits=[(sDV, nm_ok)])
    last = max(w5, wz)
    for k8 in range(8):
        m0 = 0
        for tt in range(NTL):
            m0 = V(lambda e, tt=tt: e.reduce_max(g_m[:, tt:tt + 1],
                                                 probs[:, tt * 32:(tt + 1) * 32], axis=AX.X),
                   waits=[(sDV, last)])
        m1 = V(lambda e, k8=k8: e.tensor_copy(vs8[:, k8 * 16:(k8 + 1) * 16], g_m[:]),
               waits=[(sDV, m0)])
        m2 = V(lambda e: e.tensor_add(g_run[:], g_run[:], g_m[:]), waits=[(sDV, m0)])
        m3 = V(lambda e, k8=k8: e.tensor_copy(ps8[:, k8 * 16:(k8 + 1) * 16], g_run[:]),
               waits=[(sDV, m2)])
        last = max(m1, m3)
        if k8 < 7:
            eq_ok = 0
            for tt in range(NTL):
                eq_ok = V(lambda e, tt=tt: e.tensor_scalar(
                    g_eq[:, tt * 32:(tt + 1) * 32], probs[:, tt * 32:(tt + 1) * 32],
                    g_m[:, tt:tt + 1], None, ALU.is_equal), waits=[(sDV, m1)])
            m5 = V(lambda e: e.scalar_tensor_tensor(probs[:], g_eq[:], -1e30, probs[:],
                                                    ALU.mult, ALU.add), waits=[(sDV, eq_ok)])
            last = max(last, m5)
    t0 = V(lambda e: e.tensor_scalar(g_m[:], ps8[:, 7 * 16:8 * 16], 0.8, None, ALU.mult),
           waits=[(sDV, last)])
    tz = V(lambda e: e.memset(g_nk[:], 1.0), waits=[(sDV, last)])
    nk_ok = tz
    for k8 in range(7):
        c_ = V(lambda e, k8=k8: e.tensor_tensor(g_dd[:], ps8[:, k8 * 16:(k8 + 1) * 16],
                                                g_m[:], ALU.is_lt), waits=[(sDV, t0), (sDV, nk_ok)])
        nk_ok = V(lambda e: e.tensor_add(g_nk[:], g_nk[:], g_dd[:]), waits=[(sDV, c_)])
    vz = V(lambda e: e.memset(g_vst[:], 0.0), waits=[(sDV, last)])
    dz = V(lambda e: e.memset(g_dd[:], 0.0), waits=[(sDV, nk_ok)])
    vst_ok, dd_ok = vz, dz
    for k8 in range(8):
        oh = V(lambda e, k8=k8: e.tensor_scalar(g_t1[:, 0:16], g_nk[:], float(k8 + 1), None,
                                                ALU.is_equal), waits=[(sDV, nk_ok), (sDV, vst_ok), (sDV, dd_ok)])
        p1 = V(lambda e, k8=k8: e.tensor_mul(g_t1[:, 16:32], vs8[:, k8 * 16:(k8 + 1) * 16],
                                             g_t1[:, 0:16]), waits=[(sDV, oh)])
        vst_ok = V(lambda e: e.tensor_add(g_vst[:], g_vst[:], g_t1[:, 16:32]),
                   waits=[(sDV, p1)])
        p2 = V(lambda e, k8=k8: e.tensor_mul(g_t1[:, 32:48], ps8[:, k8 * 16:(k8 + 1) * 16],
                                             g_t1[:, 0:16]), waits=[(sDV, oh)])
        dd_ok = V(lambda e: e.tensor_add(g_dd[:], g_dd[:], g_t1[:, 32:48]),
                  waits=[(sDV, p2)])
    rd_ok = V(lambda e: e.reciprocal(g_nk[:], g_dd[:]), waits=[(sDV, dd_ok)])
    ge_ok = 0
    for tt in range(NTL):
        ge_ok = V(lambda e, tt=tt: e.tensor_scalar(
            g_eq[:, tt * 32:(tt + 1) * 32], probs0[:, tt * 32:(tt + 1) * 32],
            g_vst[:, tt:tt + 1], None, ALU.is_ge), waits=[(sDV, vst_ok), (sDV, w5)])
    tb_ = V(lambda e: e.tensor_mul(comb[:], probs0[:], g_eq[:]), waits=[(sDV, ge_ok)])
    comb_ok = 0
    for tt in range(NTL):
        comb_ok = V(lambda e, tt=tt: e.tensor_scalar(
            comb[:, tt * 32:(tt + 1) * 32], comb[:, tt * 32:(tt + 1) * 32],
            g_nk[:, tt:tt + 1], None, ALU.mult), waits=[(sDV, tb_), (sDV, rd_ok)])
    if taps:
        emit("gpsimd", lambda e: e.dma_start(
            tap_x1.rearrange("(k p) t -> p k t", p=128),
            xT[:].rearrange("p (k t) -> p k t", t=2048)),
            waits=[(sDV, x1_done)], inc=qDG, inc_n=16)
        emit("gpsimd", lambda e: e.dma_start(tap_comb, comb[:]),
             waits=[(sDV, comb_ok)], inc=qDG, inc_n=16)
    barrier()

    gu_done_pe = 0
    down_done_pe = 0
    x1w_users_pe = 0
    prev_store = [0] * TT1
    x1kt_used = [0, 0]
    pfree = {}
    g_done_pe = 0
    preload = {}

    for p_ in range(NPASS):
        if p_ in preload:
            x1w_ok, pre_a, pre_b = preload.pop(p_)
        else:
            pre_a = pre_b = None
            x1w_ok = Y(lambda e, p_=p_: e.dma_start(
                x1w[:].rearrange("p (k t) -> p k t", t=TW),
                x1t_sp.rearrange("(k p) t -> p k t", p=128)[:, :, p_ * TW:(p_ + 1) * TW]),
                waits=[(qDH, sp1)])
        yz = None
        for m_ in range(TT1):
            wws = [(sST, prev_store[m_])] if prev_store[m_] else []
            yz = V(lambda e, m_=m_: e.memset(yacc[:, m_ * 2048:(m_ + 1) * 2048], 0.0),
                   waits=wws)
        for e_ in range(E):
            if e_ == 0 and pre_a is not None:
                wgA_ev, wgB_ev = pre_a, pre_b
            else:
                wwsA = [(qDG, castgu[e_]), (sPE, g_done_pe)]
                wwsB = [(qDG, castgu[e_]), (sPE, gu_done_pe)]
                wgA_ev = Y(lambda s_, e_=e_: s_.dma_start(
                    wgut[:, 0:12288],
                    wgu_bf[e_, 0].rearrange("p k m -> p (k m)")), waits=wwsA)
                wgB_ev = Y(lambda s_, e_=e_: s_.dma_start(
                    wgut[:, 12288:24576],
                    wgu_bf[e_, 1].rearrange("p k m -> p (k m)")), waits=wwsB)
            wd_ev = Y(lambda s_, e_=e_: s_.dma_start(
                wdt[:], wd_bf[e_].rearrange("p k m -> p (k m)")),
                waits=[(qDG, castwd[e_]), (sPE, down_done_pe)])
            gs_mark = {}
            h_ok = 0
            for iu in range(6):
                for tc in range(TC5):
                    slot = (iu * TC5 + tc) % 3
                    def mmg(t_, iu=iu, tc=tc, slot=slot):
                        ins = None
                        for cc in range(16):
                            ins = t_.matmul(
                                psF[slot][:, 0:512],
                                wgut[:, cc * 768 + iu * 128: cc * 768 + (iu + 1) * 128],
                                x1w[:, cc * TW + tc * 512: cc * TW + tc * 512 + 512],
                                start=(cc == 0), stop=(cc == 15))
                        return ins
                    wws = [(qDH, max(wgA_ev, x1w_ok))]
                    if slot in pfree:
                        wws.append(pfree[slot])
                    mmgm = T(mmg, waits=wws)
                    gs_mark[(iu, tc)] = S(lambda s_, iu=iu, tc=tc, slot=slot: s_.activation(
                        gsilf[:, iu * TW + tc * 512: iu * TW + tc * 512 + 512],
                        psF[slot][:, 0:512], AF.Silu), waits=[(sPE, mmgm)])
                    pfree[slot] = (sAC, gs_mark[(iu, tc)])
            g_done_pe = sPE.n
            for iu in range(6):
                for tc in range(TC5):
                    slot2 = 3 + (iu * TC5 + tc) % 3
                    def mmu(t_, iu=iu, tc=tc, slot2=slot2):
                        ins = None
                        for cc in range(16):
                            ins = t_.matmul(
                                psF[slot2][:, 0:512],
                                wgut[:, 12288 + cc * 768 + iu * 128: 12288 + cc * 768 + (iu + 1) * 128],
                                x1w[:, cc * TW + tc * 512: cc * TW + tc * 512 + 512],
                                start=(cc == 0), stop=(cc == 15))
                        return ins
                    wws = [(qDH, max(wgB_ev, x1w_ok))]
                    if slot2 in pfree:
                        wws.append(pfree[slot2])
                    mmum = T(mmu, waits=wws)
                    ue = S(lambda s_, iu=iu, tc=tc, slot2=slot2: s_.copy(
                        hbuf[:, iu * TW + tc * 512: iu * TW + tc * 512 + 512],
                        psF[slot2][:, 0:512]), waits=[(sPE, mmum)])
                    h_ok = V(lambda v_, iu=iu, tc=tc: v_.tensor_mul(
                        hbuf[:, iu * TW + tc * 512: iu * TW + tc * 512 + 512],
                        hbuf[:, iu * TW + tc * 512: iu * TW + tc * 512 + 512],
                        gsilf[:, iu * TW + tc * 512: iu * TW + tc * 512 + 512]),
                        waits=[(sAC, max(ue, gs_mark[(iu, tc)]))])
                    pfree[slot2] = (sDV, h_ok)
            gu_done_pe = sPE.n
            x1w_users_pe = sPE.n
            def p7_tile(ttile, p_):
                tb = ttile % 2
                yt = yacc[:, ttile * 2048:(ttile + 1) * 2048]
                wws = [(sST, x1k_ok)]
                if x1kt_used[tb]:
                    wws.append((sDV, x1kt_used[tb]))
                ld = Y(lambda e, p_=p_, ttile=ttile, tb=tb: e.dma_start(
                    x1kt[tb][:],
                    x1k_sp[p_ * TW + ttile * 128: p_ * TW + (ttile + 1) * 128, :]),
                    waits=wws)
                yready = sDV.n
                c1 = V(lambda e, yt=yt, tb=tb: e.tensor_add(yt, yt, x1kt[tb][:]),
                       waits=[(qDH, ld), (sDV, yready)])
                c2 = V(lambda e, yt=yt: e.reduce_sum(l_s1[:], yt, axis=AX.X),
                       waits=[(sDV, c1)])
                c3 = V(lambda e, yt=yt, tb=tb: e.tensor_mul(x1kt[tb][:], yt, yt),
                       waits=[(sDV, c1)])
                c4 = V(lambda e, tb=tb: e.reduce_sum(l_s2[:], x1kt[tb][:], axis=AX.X),
                       waits=[(sDV, c3)])
                x1kt_used[tb] = c4
                c5 = V(lambda e: e.tensor_scalar(l_mean[:], l_s1[:], 1.0 / 2048, None,
                                                 ALU.mult), waits=[(sDV, c2)])
                c6 = V(lambda e: e.tensor_mul(l_msq[:], l_mean[:], l_mean[:]),
                       waits=[(sDV, c5)])
                c7 = V(lambda e: e.tensor_scalar(l_rstd[:], l_s2[:], 1.0 / 2048, None,
                                                 ALU.mult), waits=[(sDV, c4)])
                c8 = V(lambda e: e.tensor_sub(l_rstd[:], l_rstd[:], l_msq[:]),
                       waits=[(sDV, c7), (sDV, c6)])
                c8b = V(lambda e: e.tensor_scalar(l_rstd[:], l_rstd[:], EPS, None,
                                                  ALU.add), waits=[(sDV, c8)])
                c9 = S(lambda e: e.activation(l_rstd[:], l_rstd[:], AF.Ln),
                       waits=[(sDV, c8b)])
                ca = S(lambda e: e.activation(l_rstd[:], l_rstd[:], AF.Exp, scale=-0.5),
                       waits=[(sAC, c9)])
                cb_ = V(lambda e, yt=yt: e.tensor_scalar(
                    yt, yt, l_mean[:, 0:1], l_rstd[:, 0:1],
                    ALU.subtract, ALU.mult), waits=[(sDV, c1), (sAC, ca)])
                prev_store[ttile] = emit(
                    "sync", lambda e, p_=p_, ttile=ttile, yt=yt: e.dma_start(
                        out_d[p_ * TW + ttile * 128: p_ * TW + (ttile + 1) * 128, :],
                        yt),
                    waits=[(sDV, cb_)], inc=sST, inc_n=16)

            last_e = (e_ == E - 1)
            for ttile in range(TT1):
                for jc in range(4):
                    slot = 3 + (ttile * 4 + jc) % 3
                    def mmd(t_, ttile=ttile, jc=jc, slot=slot):
                        ins = None
                        for ic in range(6):
                            ins = t_.matmul(
                                psF[slot][:, 0:512],
                                hbuf[:, ic * TW + ttile * 128: ic * TW + (ttile + 1) * 128],
                                wdt[:, ic * 2048 + jc * 512: ic * 2048 + jc * 512 + 512],
                                start=(ic == 0), stop=(ic == 5))
                        return ins
                    wws = [(sDV, h_ok), (qDH, wd_ev)]
                    if slot in pfree:
                        wws.append(pfree[slot])
                    mmdm = T(mmd, waits=wws)
                    ttg = p_ * TT1 + ttile
                    ya = V(lambda v_, ttile=ttile, jc=jc, slot=slot, ttg=ttg, e_=e_:
                           v_.scalar_tensor_tensor(
                               yacc[:, ttile * 2048 + jc * 512: ttile * 2048 + jc * 512 + 512],
                               psF[slot][:, 0:512],
                               comb[:, ttg * 32 + e_: ttg * 32 + e_ + 1],
                               yacc[:, ttile * 2048 + jc * 512: ttile * 2048 + jc * 512 + 512],
                               ALU.mult, ALU.add),
                           waits=[(sPE, mmdm), (sDV, max(yz, comb_ok))])
                    pfree[slot] = (sDV, ya)
                if last_e:
                    if ttile == 0 and p_ + 1 < NPASS:
                        x1w_next = Y(lambda e, p_=p_: e.dma_start(
                            x1w[:].rearrange("p (k t) -> p k t", t=TW),
                            x1t_sp.rearrange("(k p) t -> p k t", p=128)
                            [:, :, (p_ + 1) * TW:(p_ + 2) * TW]),
                            waits=[(qDH, sp1), (sPE, gu_done_pe)])
                        pa = Y(lambda s_: s_.dma_start(
                            wgut[:, 0:12288],
                            wgu_bf[0, 0].rearrange("p k m -> p (k m)")),
                            waits=[(qDG, castgu[0]), (sPE, g_done_pe)])
                        pb = Y(lambda s_: s_.dma_start(
                            wgut[:, 12288:24576],
                            wgu_bf[0, 1].rearrange("p k m -> p (k m)")),
                            waits=[(qDG, castgu[0]), (sPE, gu_done_pe)])
                        preload[p_ + 1] = (x1w_next, pa, pb)
                    p7_tile(ttile, p_)
            down_done_pe = sPE.n

    emit_waits_only("sync", [(sST, sST.n)])

    block = ctx.enter_context(nc.Block())
    block.tensor(lambda t: [f(t) for f in ev["tensor"]])
    block.vector(lambda v: [f(v) for f in ev["vector"]])
    block.scalar(lambda s: [f(s) for f in ev["scalar"]])
    block.sync(lambda s: [f(s) for f in ev["sync"]])
    block.gpsimd(lambda g: [f(g) for f in ev["gpsimd"]])
    ctx.close()
    return nc


_NC_CACHE = None


def kernel(L0_attn_x, L0_attn_Wqkv, L0_attn_bqkv, L0_attn_Wout, L0_attn_bout,
           L0_attn_slopes, L0_ffn_gate_weight, L0_ffn_gate_up_proj,
           L0_ffn_down_proj):
    global _NC_CACHE
    x = np.asarray(L0_attn_x, np.float32)
    wqkv = np.asarray(L0_attn_Wqkv, np.float32)
    bqkv = np.ascontiguousarray(np.asarray(L0_attn_bqkv, np.float32))
    wout = np.asarray(L0_attn_Wout, np.float32)
    bout = np.ascontiguousarray(np.asarray(L0_attn_bout, np.float32))
    slopes = np.ascontiguousarray(np.asarray(L0_attn_slopes, np.float32))
    gw = np.asarray(L0_ffn_gate_weight, np.float32)
    wgu = np.asarray(L0_ffn_gate_up_proj, np.float32)
    wd = np.asarray(L0_ffn_down_proj, np.float32)

    wkv_r = np.ascontiguousarray(
        wqkv[:, 2048:].reshape(16, 128, 8, 512).transpose(2, 1, 0, 3))
    wqkv_h = np.ascontiguousarray(
        wqkv.reshape(16, 128, 3, 16, 128).transpose(3, 1, 0, 2, 4)).reshape(
        16, 128, 16, 384)
    wout_h = np.ascontiguousarray(
        wout.reshape(16, 128, 2, 8, 128).transpose(2, 3, 1, 0, 4))
    gw_t = np.ascontiguousarray(gw.reshape(32, 16, 128).transpose(2, 1, 0))
    wgu_p = np.ascontiguousarray(
        wgu.reshape(E, 2, 768, 16, 128).transpose(0, 1, 4, 3, 2))
    wd_p = np.ascontiguousarray(
        wd.reshape(E, 2048, 6, 128).transpose(0, 3, 2, 1))

    if _NC_CACHE is None:
        _NC_CACHE = build_nc()
    nc = _NC_CACHE

    zeros = np.zeros((C, NT), np.float32)
    flag0 = np.zeros((128, 1), np.float32)
    flag1 = np.ones((128, 1), np.float32)
    in_maps = []
    for c in range(8):
        b, half = c // 2, c % 2
        x_mine = np.ascontiguousarray(x[b, half * NT:(half + 1) * NT, :].T)
        x_other = np.ascontiguousarray(x[b, 0:NT, :].T) if half == 1 else zeros
        in_maps.append(dict(x_t=x_mine, xo_t=x_other, wkv_r=wkv_r,
                            wqkv_h=wqkv_h, bqkv=bqkv, wout_h=wout_h,
                            bout=bout, slopes=slopes, gw_t=gw_t, wgu_p=wgu_p,
                            wd_p=wd_p, cflag=flag1 if half == 1 else flag0))
    res = run_bass_kernel_spmd(nc, in_maps, core_ids=list(range(8)))
    out = np.empty((B, N, C), np.float32)
    for c in range(8):
        b, half = c // 2, c % 2
        out[b, half * NT:(half + 1) * NT, :] = res.results[c]["out"]
    return out
